# version 04
# baseline (speedup 1.0000x reference)
import numpy as np
import ml_dtypes

import concourse.bass as bass
import concourse.mybir as mybir
from concourse.bass_utils import run_bass_kernel_spmd

F32 = mybir.dt.float32
BF16 = mybir.dt.bfloat16
AF = mybir.ActivationFunctionType
AL = mybir.AluOpType
AX = mybir.AxisListType
bfdt = ml_dtypes.bfloat16

B, N, D = 2, 8192, 1024
H, DH, M, ITERS, KW = 16, 64, 256, 6, 13
E, HF = 8, 256
LN_EPS = 1e-5
TC = 2048
NT = 18
TLOC = 16
GROUPS = [[0, 1, 2, 3], [4, 5, 6, 7]]
QSCALE = DH ** -0.5
CHUNKS = [(1, 4), (5, 4), (9, 4), (13, 4), (0, 1), (17, 1)]
ENGS = ("tensor", "vector", "scalar", "gpsimd", "sync")


class Op:
    __slots__ = ("eng", "emit", "deps", "count", "inc", "asyn", "key")

    def __init__(self, eng, emit, deps, count, inc, asyn, key):
        self.eng, self.emit, self.deps, self.count, self.inc, self.asyn, \
            self.key = eng, emit, deps, count, inc, asyn, key


SEMKEYS = [(e, k) for e in ENGS for k in ("c", "d")]


class Prog:

    def __init__(self, nc):
        self.nc = nc
        self.sem = {k: nc.alloc_semaphore(f"sem_{k[0]}_{k[1]}") for k in SEMKEYS}
        self.total = {k: 0 for k in SEMKEYS}
        self.pend = {e: [] for e in ENGS}
        self.water = {c: {k: 0 for k in SEMKEYS} for c in ENGS}
        self.serial = False
        self.last = {e: None for e in ENGS}

    def op(self, eng, emit, deps=(), dma=False, asyn=None):
        key = (eng, "d" if dma else "c")
        k = 16 if dma else 1
        self.total[key] += k
        dl = [d for d in deps if d is not None]
        if self.serial and not dma and eng in ("vector", "scalar", "gpsimd") \
                and self.last[eng] is not None:
            dl.append(self.last[eng])
        o = Op(eng, emit, dl, self.total[key],
               k, dma if asyn is None else asyn, key)
        self.pend[eng].append(o)
        self.last[eng] = o
        return o

    def flush(self, blk):
        for eng in ENGS:
            ops = self.pend[eng]
            if not ops:
                continue

            def body(e, eng=eng, ops=list(ops)):
                for o in ops:
                    for d in o.deps:
                        if d.count > self.water[eng][d.key]:
                            e.wait_ge(self.sem[d.key], d.count)
                            self.water[eng][d.key] = d.count
                    o.emit(e).then_inc(self.sem[o.key], o.inc)
            getattr(blk, eng)(body)
            self.pend[eng] = []


def _esz(dt):
    return 2 if dt == BF16 else 4


class SbufMap:
    def __init__(self, nc, base=16512, limit=229300):
        self.nc = nc
        self.off = base
        self.limit = limit

    def alloc(self, name, shape, dtype):
        self.off = (self.off + 63) // 64 * 64
        t = self.nc.alloc_sbuf_tensor_at(name, shape, dtype, offset=self.off)
        self.off += int(np.prod(shape[1:])) * _esz(dtype)
        assert self.off <= self.limit, f"SBUF overflow at {name}: {self.off}"
        return t

    def group(self, base, specs):
        off = base
        res = []
        for name, shape, dtype in specs:
            off = (off + 63) // 64 * 64
            res.append(self.nc.alloc_sbuf_tensor_at(name, shape, dtype, offset=off))
            off += int(np.prod(shape[1:])) * _esz(dtype)
        assert off <= self.limit, f"overlay overflow at {name}: {off}"
        return res, off


def _constants():
    c = {}
    c["id_f32"] = np.eye(128, dtype=np.float32)
    c["id_bf"] = np.eye(128, dtype=bfdt)
    c["ones_bf"] = np.ones((1, 128), dtype=bfdt)
    g = np.zeros((128, 4), dtype=bfdt)
    for j in range(4):
        g[j * 32:(j + 1) * 32, j] = 1.0
    c["g_bf"] = g
    Eb = np.zeros((13, 3 * 16384), dtype=bfdt)
    p_i, q_i = np.meshgrid(np.arange(128), np.arange(128), indexing="ij")
    for pat in range(3):
        kk = q_i - p_i - 128 * (pat - 1) + 6
        ok = (kk >= 0) & (kk < 13)
        Eb[kk[ok], pat * 16384 + (p_i * 128 + q_i)[ok]] = 1.0
    c["ebas"] = Eb

    def packI(s):
        m = np.zeros((128, 512), dtype=bfdt)
        for mo in range(2):
            np.fill_diagonal(m[:, mo * 256 + mo * 128: mo * 256 + mo * 128 + 128], s)
        return m
    c["i7"], c["i15"], c["i13q"] = packI(7.0), packI(15.0), packI(13.0 * 0.25)
    e6 = np.zeros((13, 16), dtype=np.float32)
    e6[6, :] = 1.0
    c["e6"] = e6
    sel8 = np.zeros((8, 8 * 128), dtype=bfdt)
    for e in range(8):
        sel8[e, e * 128:(e + 1) * 128] = 1.0
    c["sel8"] = sel8
    return c


def build_program(debug=(), stop_after=None):
    nc = bass.Bass("TRN2", target_bir_lowering=False,
                   detect_race_conditions=False)

    xin = nc.dram_tensor("xin", [NT * 128, D], F32, kind="ExternalInput")
    hmask = nc.dram_tensor("hmask", [128, 2], F32, kind="ExternalInput")
    hsel = nc.dram_tensor("hsel", [128, 4], F32, kind="ExternalInput")
    wqkv = nc.dram_tensor("wqkv", [D, 3 * D], F32, kind="ExternalInput")
    wconv = nc.dram_tensor("wconv", [H, KW], F32, kind="ExternalInput")
    wfc = nc.dram_tensor("wfc", [D, D], F32, kind="ExternalInput")
    bfc = nc.dram_tensor("bfc", [1, D], F32, kind="ExternalInput")
    wg = nc.dram_tensor("wg", [E, D], F32, kind="ExternalInput")
    lns = nc.dram_tensor("lns", [E, D], F32, kind="ExternalInput")
    lnb = nc.dram_tensor("lnb", [E, D], F32, kind="ExternalInput")
    w1 = nc.dram_tensor("w1", [E, D, HF], F32, kind="ExternalInput")
    b1 = nc.dram_tensor("b1", [E, HF], F32, kind="ExternalInput")
    w2 = nc.dram_tensor("w2", [E, HF, D], F32, kind="ExternalInput")
    b2 = nc.dram_tensor("b2", [E, D], F32, kind="ExternalInput")
    out = nc.dram_tensor("out", [TC, D], F32, kind="ExternalOutput")

    dbg = {}

    def dbg_t(name, shape, dtype=F32):
        if name in debug:
            dbg[name] = nc.dram_tensor("dbg_" + name, list(shape), dtype,
                                       kind="ExternalOutput")
        return dbg.get(name)

    q_d = nc.dram_tensor("q_d", [D, NT * 128], BF16)
    xh_d = nc.dram_tensor("xh_d", [D, TC], BF16)
    y_d = nc.dram_tensor("y_d", [TC, D], BF16)
    w1p_d = nc.dram_tensor("w1p_d", [E * D, HF], BF16)
    cflat_d = nc.dram_tensor("cflat_d", [16, 3 * 16384], BF16)
    lmin_d = nc.dram_tensor("lmin_d", [64, 2048], BF16)
    lmout_d = nc.dram_tensor("lmout_d", [256, 2048], BF16)
    z1in_d = nc.dram_tensor("z1in_d", [H * 256, 65], F32)
    z1out_d = nc.dram_tensor("z1out_d", [H * 256, 65], F32)
    invin_d = nc.dram_tensor("invin_d", [4 * 256, 256], BF16)
    invout_d = nc.dram_tensor("invout_d", [H * 256, 256], BF16)

    Cs = _constants()
    id_f32_c = nc.inline_tensor(Cs["id_f32"], name="id_f32_c")
    id_bf_c = nc.inline_tensor(np.asarray(Cs["id_bf"]), name="id_bf_c")
    ones_bf_c = nc.inline_tensor(np.asarray(Cs["ones_bf"]), name="ones_bf_c")
    g_bf_c = nc.inline_tensor(np.asarray(Cs["g_bf"]), name="g_bf_c")
    ebas_c = nc.inline_tensor(np.asarray(Cs["ebas"]), name="ebas_c")
    i7_c = nc.inline_tensor(np.asarray(Cs["i7"]), name="i7_c")
    i15_c = nc.inline_tensor(np.asarray(Cs["i15"]), name="i15_c")
    i13q_c = nc.inline_tensor(np.asarray(Cs["i13q"]), name="i13q_c")
    sel8_c = nc.inline_tensor(np.asarray(Cs["sel8"]), name="sel8_c")
    e6_c = nc.inline_tensor(Cs["e6"], name="e6_c")

    sm = SbufMap(nc)
    x_bf = sm.alloc("x_bf", [128, NT * 1024], BF16)
    c_sb = sm.alloc("c_sb", [128, H * 3 * 128], BF16)
    z1a = sm.alloc("z1a", [128, H * 2 * 65], F32)
    qlT = sm.alloc("qlT", [128, 8 * 256], BF16)
    klT = sm.alloc("klT", [128, 8 * 256], BF16)
    zaug = sm.alloc("zaug", [128, H * 2 * 65], BF16)
    id_f = sm.alloc("id_f", [128, 128], F32)
    id_b = sm.alloc("id_b", [128, 128], BF16)
    ones_b = sm.alloc("ones_b", [1, 128], BF16)
    onec_b = sm.alloc("onec_b", [128, 1], BF16)
    g_b = sm.alloc("g_b", [128, 4], BF16)
    i7_s = sm.alloc("i7_s", [128, 512], BF16)
    i15_s = sm.alloc("i15_s", [128, 512], BF16)
    i13_s = sm.alloc("i13_s", [128, 512], BF16)
    hm_s = sm.alloc("hm_s", [128, 2], F32)
    hsel_s = sm.alloc("hsel_s", [128, 4], F32)
    stat = sm.alloc("stat", [128, 13 * NT], F32)
    cols2 = sm.alloc("cols2", [128, 2 * NT], F32)
    rows_b = sm.alloc("rows_b", [2 * NT, 128], BF16)
    rows1_b = sm.alloc("rows1_b", [1, 2 * NT * 128], BF16)
    sel8_s = sm.alloc("sel8_s", [8, 8 * 128], BF16)
    dpair = sm.alloc("dpair", [128, 256], BF16)
    gw_s = sm.alloc("gw_s", [128, 4 * TLOC], BF16)
    nmc_b = sm.alloc("nmc_b", [128, TLOC], BF16)
    mrow_b = sm.alloc("mrow_b", [1, 4 * TLOC], BF16)
    wgT_s = sm.alloc("wgT_s", [128, 64], BF16)
    mask_f = sm.alloc("mask_f", [8, TLOC * 128], BF16)
    b2_s = sm.alloc("b2_s", [8, D], BF16)
    bfc_s = sm.alloc("bfc_s", [1, D], BF16)
    cb_s = sm.alloc("cb_s", [128, 16], F32)
    lns_s = sm.alloc("lns_s", [128, 64], F32)
    lnb_s = sm.alloc("lnb_s", [128, 64], F32)
    b1_s = sm.alloc("b1_s", [128, 16], F32)
    wc_s = sm.alloc("wc_s", [16, 16], F32)
    waug_f = sm.alloc("waug_f", [13, 16], F32)
    e6_s = sm.alloc("e6_s", [13, 16], F32)
    waug_b = sm.alloc("waug_b", [13, 16], BF16)
    lgt_s = sm.alloc("lgt_s", [8, 512], F32)
    lgtT_s = sm.alloc("lgtT_s", [128, 10], F32)
    mtm_s = sm.alloc("mtm_s", [128, 8], F32)
    xnlm_s = sm.alloc("xnlm_s", [128, 512], BF16)
    lmst_s = sm.alloc("lmst_s", [128, 1024], BF16)
    nrm_s = sm.alloc("nrm_s", [128, 4], F32)
    nrow_s = sm.alloc("nrow_s", [1, 260], F32)
    nrow2_s = sm.alloc("nrow2_s", [1, 260], F32)
    d2c = sm.alloc("d2c", [128, 32], F32)
    stg_s = sm.alloc("stg_s", [128, 2 * 1024], BF16)
    rcc_b = sm.alloc("rcc_b", [1, 4], BF16)
    rec6_s = sm.alloc("rec6_s", [128, 8], F32)
    zro_s = sm.alloc("zro_s", [128, 512], BF16)

    base0 = sm.off
    (wqkv_s,), e1 = sm.group(base0, [("wqkv_s", [128, 8 * 3072], BF16)])
    (w2_s, mg_s), e2 = sm.group(base0, [("w2_s", [128, 16 * 1024], BF16),
                                        ("mg_s", [128, 16 * 512], BF16)])
    (wfc_s, qc_s, exp1_s, yb_s, r_s, outs_s), e3 = sm.group(
        base0, [("wfc_s", [128, 8 * 1024], BF16),
                ("qc_s", [128, 8 * 512], BF16),
                ("exp1_s", [128, 2 * 512], BF16),
                ("yb_s", [128, 2 * 1024], BF16),
                ("r_s", [128, 2 * 1024], BF16),
                ("outs_s", [128, 2 * 1024], F32)])
    base1 = max(e1, e2, e3)
    (w1_s, w1p_s0), f1 = sm.group(base1, [("w1_s", [128, 2 * 2048], F32),
                                          ("w1p_s0", [128, 2 * 2048], BF16)])
    (wg_s, lns_r, lnb_r, b1_r), f6 = sm.group(
        base1, [("wg_s", [8, 1024], F32), ("lns_r", [8, 1024], F32),
                ("lnb_r", [8, 1024], F32), ("b1_r", [8, 256], F32)])
    (e2all, a_nw, e2_nw, s_nw, z_nw, y_nw, x1_nw, x2_nw, pt_nw), f4 = sm.group(
        base1, [("e2all", [128, H * 512], BF16),
                ("a_nw", [128, 512], BF16),
                ("e2_nw", [128, 512], F32),
                ("s_nw", [128, 512], BF16),
                ("z_nw", [128, 512], BF16),
                ("y_nw", [128, 512], BF16),
                ("x1_nw", [128, 512], BF16),
                ("x2_nw", [128, 512], BF16),
                ("pt_nw", [128, 512], BF16)])
    (xn_s, xh_s, k_s, v_s, exp3_s), f2 = sm.group(
        base1, [("xn_s", [128, 8 * 512], BF16),
                ("xh_s", [128, 8 * 512], BF16),
                ("k_s", [128, 8 * 512], BF16),
                ("v_s", [128, 4 * 1040], BF16),
                ("exp3_s", [128, 2 * 512], BF16)])
    w1p_s = xn_s
    (invT_s, a3v_s), f5 = sm.group(
        base1, [("invT_s", [128, H * 2 * 256], BF16),
                ("a3v_s", [128, H * 2 * 64], BF16)])
    (on_s,), f3 = sm.group(base1, [("on_s", [128, NT * 1024], BF16)])
    base2 = max(f1, f2, f3, f4, f5, f6)
    (lmag_s, lmg_s, ebas_s, cfb_s), gg1 = sm.group(
        base2, [("lmag_s", [64, 2048], BF16),
                ("lmg_s", [128, 2048], BF16),
                ("ebas_s", [13, 2048], BF16),
                ("cfb_s", [16, 2048], BF16)])
    assert gg1 <= 229300, gg1

    (IX_SX, IX_SQ, IX_MU, IX_VAR, IX_R1, IX_R2, IX_R12, IX_M1R1, IX_M1R12,
     IX_R132, IX_NM132, IX_T0, IX_T1) = [i * NT for i in range(13)]

    def stc(ix, t):
        return stat[:, ix + t:ix + t + 1]

    P = Prog(nc)
    P.serial = True
    O = P.op

    def _stop(tag, blk, dep_ops):
        if stop_after != tag:
            return False
        d = O("sync", lambda e: e.dma_start(out=out[:, :], in_=xin[0:TC, :]),
              deps=dep_ops, dma=True)
        O("sync", lambda e: e.nop(), deps=[d])
        P.flush(blk)
        return True

    with (
        nc.psum_tensor("p0_d1", [128, 512], F32) as p0_d1,
        nc.psum_tensor("p0_d2", [128, 512], F32) as p0_d2,
        nc.psum_tensor("p0_d3", [128, 512], F32) as p0_d3,
        nc.psum_tensor("p0_d4", [128, 512], F32) as p0_d4,
        nc.Block("s0a") as blk,
    ):
        cst = {}
        for nm, dst, src in [("idf", id_f, id_f32_c), ("idb", id_b, id_bf_c),
                             ("ones", ones_b, ones_bf_c), ("g", g_b, g_bf_c),
                             ("i7", i7_s, i7_c), ("i15", i15_s, i15_c),
                             ("i13", i13_s, i13q_c), ("hm", hm_s, hmask),
                             ("hsel", hsel_s, hsel)]:
            cst[nm] = O("sync", lambda e, d=dst, s2=src: e.dma_start(
                out=d[:, :], in_=s2[:, :]), dma=True)
        cst["wc"] = O("sync", lambda e: e.dma_start(
            out=wc_s[0:16, 0:13], in_=wconv[:, :]), dma=True)
        cst["sel8"] = O("sync", lambda e: e.dma_start(
            out=sel8_s[0:8, :], in_=sel8_c[:, :]), dma=True)
        cst["e6"] = O("sync", lambda e: e.dma_start(
            out=e6_s[0:13, :], in_=e6_c[:, :]), dma=True)
        cst["lns"] = O("sync", lambda e: e.dma_start(
            out=lns_r[0:8, :], in_=lns[:, :]), dma=True)
        cst["lnb"] = O("sync", lambda e: e.dma_start(
            out=lnb_r[0:8, :], in_=lnb[:, :]), dma=True)
        cst["b1"] = O("sync", lambda e: e.dma_start(
            out=b1_r[0:8, :], in_=b1[:, :]), dma=True)

        xbf_ld = O("gpsimd", lambda e: e.dma_start(
            out=x_bf.ap().rearrange("p (t f) -> p t f", f=1024),
            in_=xin.ap().rearrange("(t p) f -> p t f", p=128)), dma=True)
        wqkv_ld = O("gpsimd", lambda e: e.dma_start(
            out=wqkv_s.ap().rearrange("p (fi c) -> p fi c", c=3072),
            in_=wqkv.ap().rearrange("(fi p) c -> p fi c", p=128)), dma=True)
        b2_ld = O("gpsimd", lambda e: e.dma_start(out=b2_s[:, :], in_=b2[:, :]),
                  dma=True)
        bfc_ld = O("gpsimd", lambda e: e.dma_start(out=bfc_s[:, :], in_=bfc[:, :]),
                   dma=True)
        wgT_ld = O("gpsimd", lambda e: e.dma_start(
            out=wg_s[0:8, :], in_=wg[:, :]), dma=True)

        zro0 = O("vector", lambda e: e.memset(zro_s[:, :], 0.0))
        sq_acts = []
        if dbg_t("xbf", [128, NT * 1024], BF16):
            O("sync", lambda e: e.dma_start(out=dbg["xbf"][:, :],
                                            in_=x_bf[:, :]), deps=[xbf_ld], dma=True)
        sq_last = None
        sqr_ops = []
        for t in range(NT):
            for hf in range(2):
                i = 2 * t + hf
                ps = p0_d1 if i % 2 == 0 else p0_d2
                dd = [xbf_ld]
                if len(sqr_ops) >= 2:
                    dd.append(sqr_ops[-2])
                ac = O("scalar", lambda e, t=t, hf=hf, ps=ps: e.activation(
                    ps[:, :], x_bf[:, t * 1024 + hf * 512:t * 1024 + (hf + 1) * 512],
                    AF.Square), deps=dd)
                sqr_ops.append(O("vector", lambda e, t=t, hf=hf, ps=ps: e.reduce_sum(
                    stc(IX_T0 + hf * NT, t), ps[:, :], axis=AX.X), deps=[ac]))
        sq_last = sqr_ops[-1]
        for t in range(NT):
            O("vector", lambda e, t=t: e.reduce_sum(
                stc(IX_SX, t), x_bf[:, t * 1024:t * 1024 + 512], axis=AX.X),
                deps=[xbf_ld])
            O("vector", lambda e, t=t: e.reduce_sum(
                stc(IX_SQ, t), x_bf[:, t * 1024 + 512:(t + 1) * 1024], axis=AX.X))
        onec = O("vector", lambda e: e.memset(onec_b[:, :], 1.0))
        O("vector", lambda e: e.tensor_add(
            stat[:, IX_SX:IX_SX + NT], stat[:, IX_SX:IX_SX + NT],
            stat[:, IX_SQ:IX_SQ + NT]))
        O("vector", lambda e: e.tensor_add(
            stat[:, IX_SQ:IX_SQ + NT], stat[:, IX_T0:IX_T0 + NT],
            stat[:, IX_T1:IX_T1 + NT]), deps=[sq_last])
        O("vector", lambda e: e.tensor_scalar_mul(
            stat[:, IX_MU:IX_MU + NT], stat[:, IX_SX:IX_SX + NT], 1.0 / D))
        O("vector", lambda e: e.tensor_mul(
            stat[:, IX_T0:IX_T0 + NT], stat[:, IX_MU:IX_MU + NT],
            stat[:, IX_MU:IX_MU + NT]))
        O("vector", lambda e: e.tensor_scalar_mul(
            stat[:, IX_T1:IX_T1 + NT], stat[:, IX_SQ:IX_SQ + NT], 1.0 / D))
        O("vector", lambda e: e.tensor_sub(
            stat[:, IX_VAR:IX_VAR + NT], stat[:, IX_T1:IX_T1 + NT],
            stat[:, IX_T0:IX_T0 + NT]))
        O("vector", lambda e: e.tensor_scalar_add(
            stat[:, IX_T0:IX_T0 + NT], stat[:, IX_VAR:IX_VAR + NT], LN_EPS))
        O("vector", lambda e: e.reciprocal(
            stat[:, IX_T1:IX_T1 + NT], stat[:, IX_T0:IX_T0 + NT]))
        O("vector", lambda e: e.tensor_mul(
            stat[:, IX_T0:IX_T0 + NT], stat[:, IX_VAR:IX_VAR + NT],
            stat[:, IX_T1:IX_T1 + NT]))
        O("vector", lambda e: e.tensor_scalar_add(
            stat[:, IX_T0:IX_T0 + NT], stat[:, IX_T0:IX_T0 + NT], LN_EPS))
        recs = O("vector", lambda e: e.reciprocal(
            stat[:, IX_T0:IX_T0 + NT], stat[:, IX_T0:IX_T0 + NT]))
        O("scalar", lambda e: e.activation(
            stat[:, IX_R1:IX_R1 + NT], stat[:, IX_T1:IX_T1 + NT], AF.Sqrt),
            deps=[recs])
        sq2 = O("scalar", lambda e: e.activation(
            stat[:, IX_R2:IX_R2 + NT], stat[:, IX_T0:IX_T0 + NT], AF.Sqrt))
        O("vector", lambda e: e.tensor_mul(
            stat[:, IX_R12:IX_R12 + NT], stat[:, IX_R1:IX_R1 + NT],
            stat[:, IX_R2:IX_R2 + NT]), deps=[sq2])
        O("vector", lambda e: e.scalar_tensor_tensor(
            stat[:, IX_M1R1:IX_M1R1 + NT], stat[:, IX_MU:IX_MU + NT], -1.0,
            stat[:, IX_R1:IX_R1 + NT], AL.mult, AL.mult))
        O("vector", lambda e: e.tensor_mul(
            stat[:, IX_M1R12:IX_M1R12 + NT], stat[:, IX_M1R1:IX_M1R1 + NT],
            stat[:, IX_R2:IX_R2 + NT]))
        O("vector", lambda e: e.tensor_scalar_mul(
            stat[:, IX_R132:IX_R132 + NT], stat[:, IX_R1:IX_R1 + NT], 1.0 / 32))
        O("vector", lambda e: e.tensor_scalar_mul(
            stat[:, IX_NM132:IX_NM132 + NT], stat[:, IX_M1R1:IX_M1R1 + NT],
            1.0 / 32))
        for t in range(NT):
            O("vector", lambda e, t=t: e.tensor_copy(
                cols2[:, 2 * t:2 * t + 1], stc(IX_M1R1, t)))
            O("vector", lambda e, t=t: e.tensor_copy(
                cols2[:, 2 * t + 1:2 * t + 2], stc(IX_M1R12, t)))
        gw_ops = []
        for tl in range(TLOC):
            O("vector", lambda e, tl=tl: e.tensor_scalar_mul(
                gw_s[:, 4 * tl:4 * tl + 4], g_b[:, :], stc(IX_R132, tl + 1)),
                deps=[cst["g"]])
            gw_ops.append(O("vector", lambda e, tl=tl: e.tensor_copy(
                nmc_b[:, tl:tl + 1], stc(IX_NM132, tl + 1))))
        stats_done = O("vector", lambda e: e.tensor_copy(
            lgtT_s[:, 1:2], stc(IX_R1, 0)))
        if dbg_t("stat", [128, 13 * NT]):
            O("sync", lambda e: e.dma_start(out=dbg["stat"][:, :],
                                            in_=stat[:, :]),
              deps=[stats_done, sq_last], dma=True)
        if _stop("s0a", blk, [stats_done, sq_last, cst["e6"]]):
            return nc, dbg
        P.flush(blk)

    with (
        nc.psum_tensor("pb_t1", [128, 128], F32) as pb_t1,
        nc.psum_tensor("pb_t2", [128, 128], F32) as pb_t2,
        nc.psum_tensor("pb_mr", [1, 4 * TLOC], F32) as pb_mr,
        nc.psum_tensor("pb_b1", [128, 128], BF16) as pb_b1,
        nc.psum_tensor("pb_b2", [128, 128], BF16) as pb_b2,
        nc.psum_tensor("pb_lm", [128, 512], F32) as pb_lm,
        nc.psum_tensor("pb_lq", [128, 512], F32) as pb_lq,
        nc.psum_tensor("pb_lk", [128, 512], F32) as pb_lk,
        nc.Block("s0b") as blk,
    ):
        tr_rows = O("tensor", lambda e: e.transpose(
            pb_t1[0:2 * NT, :], cols2[:, :], id_f[:, :]),
            deps=[stats_done, cst["idf"]])
        mr_mm = None
        for tl in range(TLOC):
            mr_mm = O("tensor", lambda e, tl=tl: e.matmul(
                pb_mr[0:1, 4 * tl:4 * tl + 4], nmc_b[:, tl:tl + 1], g_b[:, :],
                start=True, stop=True), deps=[gw_ops[tl]])
        rows_ev0 = O("vector", lambda e: e.tensor_copy(
            rows_b[:, :], pb_t1[0:2 * NT, :]), deps=[tr_rows])
        rows_ev = O("gpsimd", lambda e: e.dma_start(
            out=rows1_b[0:1, :], in_=rows_b[:, :]), deps=[rows_ev0], dma=True)
        mrow_ev = O("vector", lambda e: e.tensor_copy(
            mrow_b[:, :], pb_mr[0:1, :]), deps=[mr_mm])

        lm_mm = None
        for tl in range(TLOC):
            tt = tl + 1
            for fi in range(8):
                O("tensor", lambda e, tt=tt, fi=fi, tl=tl: e.matmul(
                    pb_lm[:, fi * 64 + 4 * tl:fi * 64 + 4 * tl + 4],
                    x_bf[:, tt * 1024 + fi * 128:tt * 1024 + (fi + 1) * 128],
                    gw_s[:, 4 * tl:4 * tl + 4], start=True, stop=False),
                    deps=[mrow_ev, rows_ev])
                lm_mm = O("tensor", lambda e, fi=fi, tl=tl: e.matmul(
                    pb_lm[:, fi * 64 + 4 * tl:fi * 64 + 4 * tl + 4],
                    ones_b[0:1, :], mrow_b[0:1, 4 * tl:4 * tl + 4],
                    start=False, stop=True))
        xnlm_ev = O("vector", lambda e: e.tensor_copy(xnlm_s[:, :], pb_lm[:, :]),
                    deps=[lm_mm])

        lmp_mm = None
        for m in range(8):
            for fi in range(8):
                O("tensor", lambda e, m=m, fi=fi: e.matmul(
                    pb_lq[:, m * 64:(m + 1) * 64],
                    wqkv_s[:, fi * 3072 + m * 128:fi * 3072 + (m + 1) * 128],
                    xnlm_s[:, fi * 64:(fi + 1) * 64],
                    start=(fi == 0), stop=(fi == 7)), deps=[xnlm_ev, wqkv_ld])
                lmp_mm = O("tensor", lambda e, m=m, fi=fi: e.matmul(
                    pb_lk[:, m * 64:(m + 1) * 64],
                    wqkv_s[:, fi * 3072 + 1024 + m * 128:fi * 3072 + 1024 + (m + 1) * 128],
                    xnlm_s[:, fi * 64:(fi + 1) * 64],
                    start=(fi == 0), stop=(fi == 7)))
        lme_q = O("scalar", lambda e: e.activation(
            lmst_s[:, 0:512], pb_lq[:, :], AF.Copy, scale=QSCALE), deps=[lmp_mm])
        lme_k = O("scalar", lambda e: e.activation(
            lmst_s[:, 512:1024], pb_lk[:, :], AF.Copy))

        lmag_ev = []
        for m in range(16):
            ps = pb_b1 if m % 2 == 0 else pb_b2
            dd = [lme_k, cst["idb"]] if m < 2 else [lmag_ev[m - 2]]
            tr = O("tensor", lambda e, m=m, ps=ps: e.transpose(
                ps[0:64, :], lmst_s[:, m * 64:(m + 1) * 64], id_b[:, :]), deps=dd)
            lmag_ev.append(O("vector", lambda e, m=m, ps=ps: e.tensor_copy(
                lmag_s[:, m * 128:(m + 1) * 128], ps[0:64, :]), deps=[tr]))
        lmin_st = O("sync", lambda e: e.dma_start(
            out=lmin_d[:, :], in_=lmag_s[:, :]), deps=[lmag_ev[-1]], dma=True)
        cc_lm = O("gpsimd", lambda e: e.collective_compute(
            "AllGather", AL.bypass, replica_groups=GROUPS,
            ins=[lmin_d.ap().opt()], outs=[lmout_d.ap().opt()]), deps=[lmin_st])
        lmb_ev = []
        lmg_lds = []
        last_tr = None
        for g2 in range(2):
            dd = [cc_lm] if g2 == 0 else [last_tr]
            lmg_lds.append(O("gpsimd", lambda e, g2=g2: e.dma_start(
                out=lmg_s[:, :],
                in_=lmout_d[g2 * 128:(g2 + 1) * 128, :]), deps=dd, dma=True))
            for fi in range(16):
                i = g2 * 16 + fi
                ps = pb_b1 if i % 2 == 0 else pb_b2
                dd = [lmg_lds[g2]] + ([lmb_ev[i - 2]] if i >= 2 else [])
                last_tr = O("tensor", lambda e, fi=fi, ps=ps: e.transpose(
                    ps[:, :], lmg_s[:, fi * 128:(fi + 1) * 128],
                    id_b[:, :]), deps=dd)
                dst = qlT if fi < 8 else klT
                fj = fi % 8
                lmb_ev.append(O("vector", lambda e, dst=dst, fj=fj, g2=g2, ps=ps:
                                e.tensor_copy(
                                    dst[:, fj * 256 + g2 * 128:fj * 256 + (g2 + 1) * 128],
                                    ps[:, :]), deps=[last_tr]))
        lm_done = lmb_ev[-1]
        if dbg_t("qlk", [128, 2 * 2048], BF16):
            O("sync", lambda e: e.dma_start(out=dbg["qlk"][:, 0:2048],
                                            in_=qlT[:, :]), deps=[lm_done], dma=True)
            O("sync", lambda e: e.dma_start(out=dbg["qlk"][:, 2048:4096],
                                            in_=klT[:, :]), deps=[lm_done], dma=True)
        wgt_evs = []
        for fi in range(8):
            ps = pb_t1 if fi % 2 == 0 else pb_t2
            dd = [wgT_ld, cst["idf"], lm_done] if fi < 2 else [wgt_evs[fi - 2]]
            tr = O("tensor", lambda e, fi=fi, ps=ps: e.transpose(
                ps[0:128, 0:8], wg_s[0:8, fi * 128:(fi + 1) * 128],
                id_f[0:8, 0:8]), deps=dd)
            wgt_evs.append(O("vector", lambda e, fi=fi, ps=ps: e.tensor_copy(
                wgT_s[:, fi * 8:(fi + 1) * 8], ps[0:128, 0:8]), deps=[tr]))
        wgT_done = wgt_evs[-1]
        vecT_evs = []
        vt_jobs = [(lns_r, lns_s, 8), (lnb_r, lnb_s, 8), (b1_r, b1_s, 2)]
        i = 0
        for srcT, dstT, nfi in vt_jobs:
            for fi in range(nfi):
                ps = pb_t1 if i % 2 == 0 else pb_t2
                dd = [cst["lns"], cst["lnb"], cst["b1"], wgT_done] if i < 2 \
                    else [vecT_evs[i - 2]]
                tr = O("tensor", lambda e, srcT=srcT, fi=fi, ps=ps: e.transpose(
                    ps[0:128, 0:8], srcT[0:8, fi * 128:(fi + 1) * 128],
                    id_f[0:8, 0:8]), deps=dd)
                vecT_evs.append(O("vector", lambda e, dstT=dstT, fi=fi, ps=ps:
                                  e.tensor_copy(dstT[:, fi * 8:(fi + 1) * 8],
                                                ps[0:128, 0:8]), deps=[tr]))
                i += 1
        vecs_done = vecT_evs[-1]
        if _stop("s0b", blk, [vecs_done, lm_done, cc_lm]):
            return nc, dbg
        P.flush(blk)

    with (
        nc.psum_tensor("pc_ce", [128, 2], F32) as pc_ce,
        nc.psum_tensor("pc_tr", [128, 128], F32) as pc_tr,
        nc.psum_tensor("pc_c", [16, 2048], F32) as pc_c,
        nc.Block("s0c") as blk,
    ):
        w1p_st = []
        ce_ev_prev = None
        w1p_scale_last = None
        for e_ in range(E):
            dd = [cst["b1"], vecs_done, wgT_done] if e_ < 2 else [w1p_st[e_ - 2]]
            w1_ld = O("scalar", lambda e, e_=e_: e.dma_start(
                out=w1_s[:, (e_ % 2) * 2048:(e_ % 2) * 2048 + 2048]
                .rearrange("p (fi c) -> p fi c", c=256),
                in_=w1[e_, :, :].rearrange("(fi p) c -> p fi c", p=128)),
                deps=dd, dma=True)
            ce_mm = None
            base = (e_ % 2) * 2048
            for mo in range(2):
                for fi in range(8):
                    dd2 = [w1_ld, vecs_done]
                    if ce_ev_prev is not None:
                        dd2.append(ce_ev_prev)
                    ce_mm = O("tensor", lambda e, base=base, mo=mo, fi=fi, e_=e_:
                              e.matmul(pc_ce[:, mo:mo + 1],
                                       w1_s[:, base + fi * 256 + mo * 128:base + fi * 256 + (mo + 1) * 128],
                                       lnb_s[:, fi * 8 + e_:fi * 8 + e_ + 1],
                                       start=(fi == 0), stop=(fi == 7)),
                              deps=dd2 if (mo == 0 and fi == 0) else [])
            O("vector", lambda e, e_=e_: e.tensor_add(
                cb_s[:, 2 * e_:2 * e_ + 1], pc_ce[:, 0:1],
                b1_s[:, e_:e_ + 1]), deps=[ce_mm])
            ce_ev_prev = O("vector", lambda e, e_=e_: e.tensor_add(
                cb_s[:, 2 * e_ + 1:2 * e_ + 2], pc_ce[:, 1:2],
                b1_s[:, 8 + e_:9 + e_]))
            last_sc = None
            for fi in range(8):
                last_sc = O("vector", lambda e, base=base, fi=fi, e_=e_:
                            e.tensor_scalar_mul(
                                w1p_s0[:, base + fi * 256:base + (fi + 1) * 256],
                                w1_s[:, base + fi * 256:base + (fi + 1) * 256],
                                lns_s[:, fi * 8 + e_:fi * 8 + e_ + 1]),
                            deps=[w1_ld, vecs_done, ce_mm])
            w1p_scale_last = last_sc
            w1p_st.append(O("sync", lambda e, e_=e_, base=base: e.dma_start(
                out=w1p_d[e_ * 1024:(e_ + 1) * 1024, :]
                .rearrange("(fi p) c -> p fi c", p=128),
                in_=w1p_s0[:, base:base + 2048]
                .rearrange("p (fi c) -> p fi c", c=256)),
                deps=[last_sc], dma=True))

        wc_tr = O("tensor", lambda e: e.transpose(
            pc_tr[0:16, 0:16], wc_s[0:16, 0:16], id_f[0:16, 0:16]),
            deps=[cst["wc"], cst["idf"]])
        O("vector", lambda e: e.tensor_copy(waug_f[0:13, 0:16], pc_tr[0:13, 0:16]),
          deps=[wc_tr])
        O("vector", lambda e: e.tensor_add(
            waug_f[0:13, 0:16], waug_f[0:13, 0:16], e6_s[0:13, 0:16]),
          deps=[cst["e6"]])
        waug_ev = O("vector", lambda e: e.tensor_copy(
            waug_b[0:13, 0:16], waug_f[0:13, 0:16]))

        cf_st = []
        eb_ld = []
        ch_mm = []
        cb_ev = []
        for ch in range(24):
            dd = [] if ch < 1 else [ch_mm[ch - 1]]
            eb_ld.append(O("scalar", lambda e, ch=ch: e.dma_start(
                out=ebas_s[0:13, :],
                in_=ebas_c[0:13, ch * 2048:(ch + 1) * 2048]), deps=dd, dma=True))
            mm = None
            for jj in range(4):
                ddd = []
                if jj == 0:
                    ddd = [eb_ld[ch], waug_ev]
                    if ch >= 1:
                        ddd.append(cb_ev[ch - 1])
                mm = O("tensor", lambda e, ch=ch, jj=jj: e.matmul(
                    pc_c[0:16, jj * 512:(jj + 1) * 512],
                    waug_b[0:13, 0:16],
                    ebas_s[0:13, jj * 512:(jj + 1) * 512],
                    start=True, stop=True), deps=ddd)
            ch_mm.append(mm)
            cb_ev.append(O("vector", lambda e: e.tensor_copy(
                cfb_s[0:16, :], pc_c[0:16, :]), deps=[mm]))
            cf_st.append(O("gpsimd", lambda e, ch=ch: e.dma_start(
                out=cflat_d[0:16, ch * 2048:(ch + 1) * 2048], in_=cfb_s[0:16, :]),
                deps=[cb_ev[-1]] + ([cf_st[-1]] if cf_st else []), dma=True))
        csb_ld = O("gpsimd", lambda e: e.dma_start(
            out=c_sb.ap().rearrange("p (h pat q) -> p h pat q", pat=3, q=128),
            in_=cflat_d.ap().rearrange("h (pat p q) -> p h pat q",
                                       pat=3, p=128)),
            deps=[cf_st[-1]], dma=True)
        if _stop("s0c", blk, [csb_ld, w1p_st[-1], ce_ev_prev]):
            return nc, dbg
        P.flush(blk)

    with (
        nc.psum_tensor("pn_a", [128, 512], F32) as pn_a,
        nc.psum_tensor("pn_b", [128, 512], F32) as pn_b,
        nc.psum_tensor("pn_y1", [128, 256], F32) as pn_y1,
        nc.psum_tensor("pn_y2", [128, 256], F32) as pn_y2,
        nc.psum_tensor("pn_cs", [128, 2], F32) as pn_cs,
        nc.psum_tensor("pn_t", [128, 128], BF16) as pn_t,
        nc.psum_tensor("pn_rc", [128, 1], F32) as pn_rc,
        nc.Block("sn") as blk,
    ):
        e2_evs = []
        for h in range(H):
            po = (h % 2) * 64
            fj = h // 2
            ps = pn_a if h % 2 == 0 else pn_b
            mm = None
            for mo in range(2):
                dd = []
                if mo == 0:
                    dd = [lm_done] if h < 2 else [e2_evs[h - 2]]
                mm = O("tensor", lambda e, po=po, fj=fj, mo=mo, ps=ps: e.matmul(
                    ps[:, mo * 256:(mo + 1) * 256],
                    qlT[po:po + 64, fj * 256 + mo * 128:fj * 256 + (mo + 1) * 128],
                    klT[po:po + 64, fj * 256:(fj + 1) * 256],
                    start=True, stop=True), deps=dd)
            ex1d = [mm] + ([e2_evs[h - 1]] if h >= 1 else
                           [w1p_st[-1], ce_ev_prev])
            O("scalar", lambda e, h=h, ps=ps: e.activation(
                e2_nw[:, 0:256], ps[:, 0:256], AF.Exp), deps=ex1d)
            ex2 = O("scalar", lambda e, h=h, ps=ps: e.activation(
                e2_nw[:, 256:512], ps[:, 256:512], AF.Exp))
            O("vector", lambda e, h=h: e.reduce_sum(
                d2c[:, 2 * h:2 * h + 1], e2_nw[:, 0:256], axis=AX.X), deps=[ex2])
            O("vector", lambda e, h=h: e.reduce_sum(
                d2c[:, 2 * h + 1:2 * h + 2], e2_nw[:, 256:512], axis=AX.X))
            O("vector", lambda e, h=h: e.tensor_add(
                d2c[:, 2 * h:2 * h + 1], d2c[:, 2 * h:2 * h + 1],
                d2c[:, 2 * h + 1:2 * h + 2]))
            O("vector", lambda e, h=h: e.reciprocal(
                d2c[:, 2 * h + 1:2 * h + 2], d2c[:, 2 * h:2 * h + 1]))
            e2_evs.append(O("vector", lambda e, h=h: e.tensor_scalar_mul(
                e2all[:, h * 512:(h + 1) * 512], e2_nw[:, :],
                d2c[:, 2 * h + 1:2 * h + 2])))

        def mm256(out_sb, out_off, lhs_sb, lhs_off, rhs_sb, rhs_off, dep0,
                  fuse=None):
            evs = []
            for mo_ in range(2):
                ps = pn_y1 if mo_ == 0 else pn_y2
                mmm = None
                for kk_ in range(2):
                    mmm = O("tensor", lambda e, mo_=mo_, kk_=kk_, ps=ps: e.matmul(
                        ps[:, :],
                        lhs_sb[:, lhs_off + kk_ * 256 + mo_ * 128:lhs_off + kk_ * 256 + (mo_ + 1) * 128],
                        rhs_sb[:, rhs_off + kk_ * 256:rhs_off + (kk_ + 1) * 256],
                        start=(kk_ == 0), stop=(kk_ == 1)),
                        deps=(dep0 + (evs[-2:-1] or [])) if (mo_ == 0 and kk_ == 0)
                        else ([evs[-2]] if (mo_ == 1 and kk_ == 0 and len(evs) >= 2)
                              else []))
                if fuse is None:
                    evs.append(O("vector", lambda e, mo_=mo_, ps=ps: e.tensor_copy(
                        out_sb[:, out_off + mo_ * 256:out_off + (mo_ + 1) * 256],
                        ps[:, :]), deps=[mmm]))
                else:
                    cstt, alpha = fuse
                    evs.append(O("vector", lambda e, mo_=mo_, ps=ps, cstt=cstt,
                                 alpha=alpha: e.scalar_tensor_tensor(
                                     out_sb[:, out_off + mo_ * 256:out_off + (mo_ + 1) * 256],
                                     ps[:, :], alpha,
                                     cstt[:, mo_ * 256:(mo_ + 1) * 256],
                                     AL.mult, AL.add), deps=[mmm]))
            return evs

        inv_sts = []
        for j in range(4):
            bl0 = O("vector", lambda e, j=j: e.tensor_scalar_mul(
                a_nw[:, :], e2all[:, j * 512:(j + 1) * 512], hsel_s[:, 0:1]),
                deps=[e2_evs[12 + j], cst["hsel"]] +
                ([inv_sts[-1]] if inv_sts else []))
            bl = bl0
            for r in range(1, 4):
                hh = r * 4 + j
                O("vector", lambda e, hh=hh, r=r: e.tensor_scalar_mul(
                    x1_nw[:, :], e2all[:, hh * 512:(hh + 1) * 512],
                    hsel_s[:, r:r + 1]))
                bl = O("vector", lambda e: e.tensor_add(
                    a_nw[:, :], a_nw[:, :], x1_nw[:, :]))
            rs1 = O("vector", lambda e: e.reduce_sum(
                nrm_s[:, 0:1], a_nw[:, 0:256], axis=AX.X))
            rs2 = O("vector", lambda e: e.reduce_sum(
                nrm_s[:, 1:2], a_nw[:, 256:512], axis=AX.X))
            cs_mm = None
            for mo in range(2):
                for kk in range(2):
                    cs_mm = O("tensor", lambda e, mo=mo, kk=kk: e.matmul(
                        pn_cs[:, mo:mo + 1],
                        a_nw[:, kk * 256 + mo * 128:kk * 256 + (mo + 1) * 128],
                        onec_b[:, :], start=(kk == 0), stop=(kk == 1)),
                        deps=[bl, onec] if (mo == 0 and kk == 0) else [])
            cs_ev = O("vector", lambda e: e.tensor_copy(
                nrm_s[:, 2:4], pn_cs[:, :]), deps=[cs_mm])
            g1_ = O("gpsimd", lambda e: e.dma_start(
                out=nrow_s[0:1, 0:256], in_=nrm_s[:, 0:2]),
                deps=[rs1, rs2, cs_ev], dma=True)
            g2_ = O("gpsimd", lambda e: e.dma_start(
                out=nrow2_s[0:1, 0:256], in_=nrm_s[:, 2:4]), dma=True)
            mx1 = O("vector", lambda e: e.reduce_max(
                nrow_s[0:1, 256:257], nrow_s[0:1, 0:256], axis=AX.X),
                deps=[g1_, g2_])
            mx2 = O("vector", lambda e: e.reduce_max(
                nrow2_s[0:1, 256:257], nrow2_s[0:1, 0:256], axis=AX.X))
            g3_ = O("gpsimd", lambda e: e.dma_start(
                out=nrow_s[0:1, 257:258], in_=nrow2_s[0:1, 256:257]),
                deps=[mx1, mx2], dma=True)
            O("vector", lambda e: e.tensor_mul(
                nrow_s[0:1, 258:259], nrow_s[0:1, 256:257], nrow_s[0:1, 257:258]),
                deps=[g3_])
            O("vector", lambda e: e.reciprocal(
                nrow_s[0:1, 259:260], nrow_s[0:1, 258:259]))
            rc_ev = O("vector", lambda e, j=j: e.tensor_copy(
                rcc_b[0:1, j:j + 1], nrow_s[0:1, 259:260]))
            rc_mm = O("tensor", lambda e, j=j: e.matmul(
                pn_rc[:, :], ones_b[0:1, :], rcc_b[0:1, j:j + 1],
                start=True, stop=True), deps=[rc_ev, cst["ones"]])
            z0_ev = None
            for kk in range(2):
                for mo in range(2):
                    dd = [rc_mm] if (kk == 0 and mo == 0) else [z0_ev]
                    tr = O("tensor", lambda e, kk=kk, mo=mo: e.transpose(
                        pn_t[:, :],
                        a_nw[:, kk * 256 + mo * 128:kk * 256 + (mo + 1) * 128],
                        id_b[:, :]), deps=dd)
                    z0_ev = O("vector", lambda e, kk=kk, mo=mo: e.tensor_scalar_mul(
                        z_nw[:, mo * 256 + kk * 128:mo * 256 + kk * 128 + 128],
                        pn_t[:, :], pn_rc[:, 0:1]), deps=[tr])
            s0_ev = None
            for kk in range(2):
                s0_ev = O("vector", lambda e, kk=kk: e.tensor_scalar_mul(
                    s_nw[:, kk * 256:(kk + 1) * 256],
                    a_nw[:, kk * 256:(kk + 1) * 256], pn_rc[:, 0:1]),
                    deps=[z0_ev] if kk == 0 else [])

            prev = [z0_ev, s0_ev]
            for it in range(ITERS):
                ev_y = mm256(y_nw, 0, s_nw, 0, a_nw, 0, prev)
                ev_x1 = mm256(x1_nw, 0, a_nw, 0, s_nw, 0, [ev_y[-1]],
                              fuse=(i7_s, -1.0))
                ev_x2 = mm256(x2_nw, 0, y_nw, 0, x1_nw, 0, [ev_x1[-1]],
                              fuse=(i15_s, -1.0))
                ev_pt = mm256(pt_nw, 0, y_nw, 0, x2_nw, 0, [ev_x2[-1]],
                              fuse=(i13_s, -0.25))
                last = it == ITERS - 1
                ev_sn = mm256(x1_nw, 0, z_nw, 0, pt_nw, 0, [ev_pt[-1]])
                if not last:
                    ev_zn = mm256(x2_nw, 0, pt_nw, 0, z_nw, 0, [ev_sn[-1]])
                    O("vector", lambda e: e.tensor_copy(s_nw[:, :], x1_nw[:, :]),
                      deps=[ev_zn[-1]])
                    prev = [O("vector", lambda e: e.tensor_copy(
                        z_nw[:, :], x2_nw[:, :]))]
                else:
                    prev = ev_sn
            inv_sts.append(O("sync", lambda e, j=j: e.dma_start(
                out=invin_d[j * 256:(j + 1) * 256, :]
                .rearrange("(kk p) c -> p kk c", p=128),
                in_=x1_nw.ap().rearrange("p (kk c) -> p kk c", c=256)),
                deps=[prev[-1]], dma=True))
        cc_inv = O("gpsimd", lambda e: e.collective_compute(
            "AllGather", AL.bypass, replica_groups=GROUPS,
            ins=[invin_d.ap().opt()], outs=[invout_d.ap().opt()]),
            deps=[inv_sts[-1]])
        if _stop("sn", blk, [cc_inv]):
            return nc, dbg
        P.flush(blk)

    with (
        nc.psum_tensor("p2_a", [128, 512], F32) as p2_a,
        nc.psum_tensor("p2_b", [128, 512], F32) as p2_b,
        nc.psum_tensor("p2_q1", [128, 512], F32) as p2_q1,
        nc.psum_tensor("p2_q2", [128, 512], F32) as p2_q2,
        nc.psum_tensor("p2_z0", [128, 65], F32) as p2_z0,
        nc.psum_tensor("p2_z1", [128, 65], F32) as p2_z1,
        nc.psum_tensor("p2_z2", [128, 65], F32) as p2_z2,
        nc.psum_tensor("p2_z3", [128, 65], F32) as p2_z3,
        nc.Block("s2") as blk,
    ):
        P.serial = False
        z1_init = O("vector", lambda e: e.memset(z1a[:, :], 0.0))
        zps = [p2_z0, p2_z1, p2_z2, p2_z3]
        q_sts, xh_sts, h1_anchor = [], [], []
        last_exp = None
        last_z1mm = [None] * 4
        last_z1add = [None] * 4
        prev_tail = [lm_done, inv_sts[-1], cc_inv]
        last_dp_reader = None
        evac_hist = []
        for ci, (t0, ntl) in enumerate(CHUNKS):
            W = ntl * 128
            is_c = ntl == 4
            ev1 = ev2 = None
            idx = 0
            for tl in range(ntl):
                t = t0 + tl
                dp_dep = [stats_done, cst["idb"]] + prev_tail
                if last_dp_reader is not None:
                    dp_dep.append(last_dp_reader)
                O("vector", lambda e, t=t: e.tensor_scalar_mul(
                    dpair[:, 0:128], id_b[:, :], stc(IX_R1, t)), deps=dp_dep)
                dp2 = O("vector", lambda e, t=t: e.tensor_scalar_mul(
                    dpair[:, 128:256], id_b[:, :], stc(IX_R12, t)))
                for fi in range(8):
                    ps = p2_a if idx % 2 == 0 else p2_b
                    depm = [dp2, rows_ev] + prev_tail
                    if len(evac_hist) >= 2:
                        depm += list(evac_hist[-2])
                    O("tensor", lambda e, t=t, fi=fi, ps=ps: e.matmul(
                        ps[:, 0:256],
                        x_bf[:, t * 1024 + fi * 128:t * 1024 + (fi + 1) * 128],
                        dpair[:, :], start=True, stop=False), deps=depm)
                    O("tensor", lambda e, t=t, ps=ps: e.matmul(
                        ps[:, 0:128], ones_b[0:1, :],
                        rows1_b[0:1, (2 * t) * 128:(2 * t + 1) * 128],
                        start=False, stop=False))
                    mm = O("tensor", lambda e, t=t, ps=ps: e.matmul(
                        ps[:, 128:256], ones_b[0:1, :],
                        rows1_b[0:1, (2 * t + 1) * 128:(2 * t + 2) * 128],
                        start=False, stop=True))
                    ev1 = O("scalar", lambda e, tl=tl, fi=fi, ps=ps, W=W:
                            e.activation(
                                xn_s[:, fi * W + tl * 128:fi * W + (tl + 1) * 128],
                                ps[:, 0:128], AF.Copy), deps=[mm])
                    ev2 = O("vector", lambda e, tl=tl, fi=fi, ps=ps, W=W:
                            e.tensor_copy(
                                xh_s[:, fi * W + tl * 128:fi * W + (tl + 1) * 128],
                                ps[:, 128:256]), deps=[ev1])
                    evac_hist.append((ev1, ev2))
                    idx += 1
                    if ci == 0 and tl == 0 and fi == 0 and _stop("s2t0", blk, [ev1, ev2]):
                        return nc, dbg
                last_dp_reader = mm
                if ci == 0 and tl == 0 and _stop("s2t1", blk, [ev1, ev2]):
                    return nc, dbg
            xn_done = [ev1, ev2]
            if ci == 0 and _stop("s2a", blk, xn_done):
                return nc, dbg
            if is_c:
                xh_dep = xn_done + ([xh_sts[-1]] if xh_sts else [])
                xh_sts.append(O("sync", lambda e, t0=t0: e.dma_start(
                    out=xh_d[:, (t0 - 1) * 128:(t0 + 3) * 128]
                    .rearrange("(fi p) c -> p fi c", p=128),
                    in_=xh_s.ap().rearrange("p (fi c) -> p fi c", c=512)),
                    deps=xh_dep, dma=True))

            qev = []
            for m in range(16):
                if m >= 8 and not is_c:
                    continue
                ps = p2_q1 if len(qev) % 2 == 0 else p2_q2
                col0 = (m % 8) * 128 + (1024 if m >= 8 else 0)
                mm = None
                for fi in range(8):
                    depm = []
                    if fi == 0:
                        depm = list(xn_done) + [wqkv_ld] + prev_tail
                        if len(qev) >= 2:
                            depm.append(qev[-2])
                    mm = O("tensor", lambda e, fi=fi, ps=ps, col0=col0, W=W:
                           e.matmul(ps[:, 0:W],
                                    wqkv_s[:, fi * 3072 + col0:fi * 3072 + col0 + 128],
                                    xn_s[:, fi * W:(fi + 1) * W],
                                    start=(fi == 0), stop=(fi == 7)), deps=depm)
                if m < 8:
                    evd = [mm] + ([q_sts[-2]] if len(q_sts) >= 2 else [])
                    ev = O("scalar", lambda e, m=m, ps=ps, W=W: e.activation(
                        stg_s[:, (m % 2) * 512:(m % 2) * 512 + W], ps[:, 0:W],
                        AF.Copy, scale=QSCALE), deps=evd)
                    q_sts.append(O("sync", lambda e, m=m, t0=t0, W=W: e.dma_start(
                        out=q_d[m * 128:(m + 1) * 128, t0 * 128:t0 * 128 + W],
                        in_=stg_s[0:128, (m % 2) * 512:(m % 2) * 512 + W]),
                        deps=[ev], dma=True))
                    qev.append(ev)
                else:
                    qev.append(O("vector", lambda e, m=m, ps=ps, W=W: e.tensor_copy(
                        k_s[:, (m - 8) * W:(m - 7) * W], ps[:, 0:W]), deps=[mm]))
            k_done = qev[-1]
            if ci == 0 and _stop("s2b", blk, [k_done, q_sts[-1]]):
                return nc, dbg

            v_evs = []
            lg_ev = None
            msk_done = []
            if is_c:
                vz = O("vector", lambda e: e.memset(
                    v_s.ap().rearrange("p (tl h c) -> p tl h c", h=16, c=65)
                    [:, :, :, 64:65], 1.0), deps=xn_done + prev_tail)
                for tl in range(4):
                    for half in range(2):
                        ps = p2_q1 if len(qev) % 2 == 0 else p2_q2
                        mm = None
                        for fi in range(8):
                            depm = []
                            if fi == 0:
                                depm = [qev[-2]]
                            mm = O("tensor", lambda e, tl=tl, half=half, fi=fi,
                                   ps=ps: e.matmul(
                                       ps[:, :],
                                       xn_s[:, fi * 512 + tl * 128:fi * 512 + (tl + 1) * 128],
                                       wqkv_s[:, fi * 3072 + 2048 + half * 512:fi * 3072 + 2048 + (half + 1) * 512],
                                       start=(fi == 0), stop=(fi == 7)), deps=depm)
                        for hh in range(8):
                            h = half * 8 + hh
                            dep_ev = [mm, vz] if hh == 0 else []
                            v_evs.append(O("vector",
                                           lambda e, tl=tl, h=h, hh=hh, ps=ps:
                                           e.tensor_copy(
                                               v_s[:, tl * 1040 + h * 65:tl * 1040 + h * 65 + 64],
                                               ps[:, hh * 64:(hh + 1) * 64]),
                                           deps=dep_ev))
                        qev.append(v_evs[-1])
                if ci == 0 and _stop("s2v", blk, [v_evs[-1]]):
                    return nc, dbg
                lg_mm = None
                for fi in range(8):
                    depm = [wgT_done, qev[-1]] if fi == 0 else []
                    lg_mm = O("tensor", lambda e, fi=fi: e.matmul(
                        p2_a[0:8, :], wgT_s[:, fi * 8:(fi + 1) * 8],
                        xn_s[:, fi * 512:(fi + 1) * 512],
                        start=(fi == 0), stop=(fi == 7)), deps=depm)
                lg_ev = O("vector", lambda e: e.tensor_copy(
                    lgt_s[0:8, :], p2_a[0:8, :]), deps=[lg_mm])
                P.serial = True
                for tl in range(4):
                    t = t0 + tl
                    trd = [lg_ev] if tl == 0 else [msk_done[-1]]
                    tr1 = O("tensor", lambda e, tl=tl: e.transpose(
                        p2_b[0:128, 0:8], lgt_s[0:8, tl * 128:(tl + 1) * 128],
                        id_f[0:8, 0:8]), deps=trd)
                    O("vector", lambda e: e.tensor_copy(
                        lgtT_s[:, 0:8], p2_b[0:128, 0:8]), deps=[tr1])
                    O("vector", lambda e: e.reduce_max(
                        lgtT_s[:, 8:9], lgtT_s[:, 0:8], axis=AX.X))
                    c3 = O("vector", lambda e: e.tensor_single_scalar(
                        mtm_s[:, 0:8], lgtT_s[:, 0:8], lgtT_s[:, 8:9], AL.is_ge))
                    tr2 = O("tensor", lambda e: e.transpose(
                        p2_b[0:8, 0:128], mtm_s[:, 0:8], id_f[:, :]), deps=[c3])
                    msk_done.append(O("vector", lambda e, t=t: e.tensor_copy(
                        mask_f[0:8, (t - 1) * 128:t * 128], p2_b[0:8, 0:128]),
                        deps=[tr2]))
                if ci == 0 and _stop("s2c", blk, [msk_done[-1]]):
                    return nc, dbg
                P.serial = False
                for hp in range(8):
                    h0, h1_ = 2 * hp, 2 * hp + 1
                    exp_ops = []
                    for tp in range(2):
                        mm_last = None
                        for tl2 in range(2):
                            tl = 2 * tp + tl2
                            depm = [k_done, lm_done, v_evs[-1], msk_done[-1]]
                            if tp == 1:
                                depm = [exp_ops[-1], last_z1mm[3]]
                            elif hp > 0:
                                depm = [last_exp, last_z1mm[3]]
                            O("tensor", lambda e, h=h0, tl=tl, tl2=tl2: e.matmul(
                                p2_q1[:, tl2 * 256:(tl2 + 1) * 256],
                                k_s[0:64,
                                    (h // 2) * 512 + tl * 128:(h // 2) * 512 + (tl + 1) * 128],
                                qlT[0:64, (h // 2) * 256:(h // 2) * 256 + 256],
                                start=True, stop=True),
                                deps=depm if tl2 == 0 else [])
                            mm_last = O("tensor", lambda e, h=h1_, tl=tl, tl2=tl2:
                                        e.matmul(
                                p2_q2[:, tl2 * 256:(tl2 + 1) * 256],
                                k_s[64:128,
                                    (h // 2) * 512 + tl * 128:(h // 2) * 512 + (tl + 1) * 128],
                                qlT[64:128, (h // 2) * 256:(h // 2) * 256 + 256],
                                start=True, stop=True))
                        exp_ops.append(O("scalar", lambda e: e.activation(
                            exp3_s[:, 0:512], p2_q1[:, :], AF.Exp),
                            deps=[mm_last]))
                        exp_ops.append(O("scalar", lambda e: e.activation(
                            exp3_s[:, 512:1024], p2_q2[:, :], AF.Exp)))
                        for tl2 in range(2):
                            tl = 2 * tp + tl2
                            for hh in range(2):
                                for mo in range(2):
                                    zi = hh * 2 + mo
                                    depz = [exp_ops[-1]]
                                    if tl == 0:
                                        depz.append(last_z1add[zi])
                                        depz.append(v_evs[-1])
                                    last_z1mm[zi] = O(
                                        "tensor", lambda e, hh=hh, mo=mo, tl=tl,
                                        tl2=tl2, hp=hp, zi=zi: e.matmul(
                                            zps[zi][:, :],
                                            exp3_s[:, hh * 512 + tl2 * 256 + mo * 128:
                                                   hh * 512 + tl2 * 256 + (mo + 1) * 128],
                                            v_s[:, tl * 1040 + (2 * hp + hh) * 65:
                                                tl * 1040 + (2 * hp + hh) * 65 + 65],
                                            start=(tl == 0), stop=(tl == 3)),
                                        deps=depz)
                    last_exp = exp_ops[-1]
                    for zi in range(4):
                        hh, mo = zi // 2, zi % 2
                        h = 2 * hp + hh
                        last_z1add[zi] = O(
                            "vector", lambda e, h=h, mo=mo, zi=zi:
                            e.scalar_tensor_tensor(
                                z1a[:, (h * 2 + mo) * 65:(h * 2 + mo + 1) * 65],
                                zps[zi][:, :], 1.0,
                                z1a[:, (h * 2 + mo) * 65:(h * 2 + mo + 1) * 65],
                                AL.mult, AL.add),
                            deps=[last_z1mm[zi], z1_init])
            prev_tail = [qev[-1]]
            last_wqkv_mm = mm
            if is_c:
                prev_tail += [last_z1add[3], v_evs[-1], msk_done[-1], last_exp]
                mask_all_done = msk_done[-1]
        if dbg_t("z1p", [128, H * 2 * 65]):
            O("sync", lambda e: e.dma_start(out=dbg["z1p"][:, :], in_=z1a[:, :]),
              deps=[o for o in last_z1add if o is not None], dma=True)
        if dbg_t("mask", [8, TLOC * 128], BF16):
            O("sync", lambda e: e.dma_start(out=dbg["mask"][:, :],
                                            in_=mask_f[0:8, :]),
              deps=[mask_all_done], dma=True)
        P.serial = True
        z1_st = O("sync", lambda e: e.dma_start(
            out=z1in_d.ap().rearrange("(h mo p) c -> p h mo c", p=128, mo=2),
            in_=z1a.ap().rearrange("p (h mo c) -> p h mo c", mo=2, c=65)),
            deps=[o for o in last_z1add if o is not None], dma=True)
        cc_z1 = O("gpsimd", lambda e: e.collective_compute(
            "AllReduce", AL.add, replica_groups=GROUPS,
            ins=[z1in_d.ap().opt()], outs=[z1out_d.ap().opt()]), deps=[z1_st])
        if _stop("s2", blk, [cc_z1, q_sts[-1], xh_sts[-1], mask_all_done]):
            return nc, dbg
        P.flush(blk)

    with (
        nc.psum_tensor("p3_h1", [128, 512], F32) as p3_h1,
        nc.psum_tensor("p3_h2", [128, 512], F32) as p3_h2,
        nc.psum_tensor("p3_mb", [128, 512], F32) as p3_mb,
        nc.psum_tensor("p3_y1", [128, 512], F32) as p3_y1,
        nc.psum_tensor("p3_y2", [128, 512], F32) as p3_y2,
        nc.Block("s3") as blk,
    ):
        P.serial = False
        w2_ld = O("gpsimd", lambda e: e.dma_start(
            out=w2_s.ap().rearrange("p (e mo c) -> p e mo c", mo=2, c=1024),
            in_=w2.ap().rearrange("e (mo p) c -> p e mo c", p=128)),
            deps=[csb_ld, last_wqkv_mm, last_exp], dma=True)
        y_sts = []
        h1_last = {0: None, 1: None}
        xh_readers_tail = None
        mb_reader = None
        yev_hist = []
        for ci in range(4):
            t0 = CHUNKS[ci][0]
            xh_dep = [xh_sts[ci]]
            if xh_readers_tail is not None:
                xh_dep.append(xh_readers_tail)
            xh_ld = O("scalar", lambda e, t0=t0: e.dma_start(
                out=xh_s.ap().rearrange("p (fi c) -> p fi c", c=512),
                in_=xh_d[:, (t0 - 1) * 128:(t0 + 3) * 128]
                .rearrange("(fi p) c -> p fi c", p=128)),
                deps=xh_dep, dma=True)
            w1p_lds = []
            gl_ops = []
            last_h1 = None
            for e_ in range(E):
                dd = [w1p_st[e_]]
                if e_ >= 2:
                    dd.append(h1_last[e_ % 2])
                elif ci > 0:
                    dd.append(h1_last[e_ % 2])
                w1p_lds.append(O("scalar", lambda e, e_=e_: e.dma_start(
                    out=w1p_s[:, (e_ % 2) * 2048:(e_ % 2) * 2048 + 2048]
                    .rearrange("p (fi c) -> p fi c", c=256),
                    in_=w1p_d[e_ * 1024:(e_ + 1) * 1024, :]
                    .rearrange("(fi p) c -> p fi c", p=128)), deps=dd, dma=True))
                mb_dep = [mask_all_done]
                if mb_reader is not None:
                    mb_dep.append(mb_reader)
                mb = O("tensor", lambda e, e_=e_, t0=t0: e.matmul(
                    p3_mb[:, :], sel8_s[0:8, e_ * 128:(e_ + 1) * 128],
                    mask_f[0:8, (t0 - 1) * 128:(t0 + 3) * 128],
                    start=True, stop=True), deps=mb_dep + [cst["sel8"]])
                base = (e_ % 2) * 2048
                for mo in range(2):
                    ps = p3_h1 if mo == 0 else p3_h2
                    mm = None
                    for fi in range(8):
                        depm = []
                        if fi == 0:
                            depm = [w1p_lds[-1], xh_ld]
                            if gl_ops:
                                depm.append(gl_ops[-1 if mo == 1 else -2]
                                            if len(gl_ops) >= 2 else gl_ops[-1])
                        mm = O("tensor", lambda e, base=base, mo=mo, fi=fi, ps=ps:
                               e.matmul(
                                   ps[:, :],
                                   w1p_s[:, base + fi * 256 + mo * 128:base + fi * 256 + (mo + 1) * 128],
                                   xh_s[:, fi * 512:(fi + 1) * 512],
                                   start=(fi == 0), stop=(fi == 7)), deps=depm)
                    gl = O("scalar", lambda e, e_=e_, mo=mo, ps=ps: e.activation(
                        mg_s[:, (e_ * 2 + mo) * 512:(e_ * 2 + mo + 1) * 512],
                        ps[:, :], AF.Gelu,
                        bias=cb_s[:, 2 * e_ + mo:2 * e_ + mo + 1]),
                        deps=[mm, ce_ev_prev] + (yev_hist[-1:] if ci > 0 and e_ == 0
                                                 and mo == 0 else []))
                    gl_ops.append(O("vector", lambda e, e_=e_, mo=mo: e.tensor_tensor(
                        mg_s[:, (e_ * 2 + mo) * 512:(e_ * 2 + mo + 1) * 512],
                        mg_s[:, (e_ * 2 + mo) * 512:(e_ * 2 + mo + 1) * 512],
                        p3_mb[:, :], AL.mult), deps=[gl, mb]))
                    h1_last[mo] = mm
                mb_reader = gl_ops[-1]
            xh_readers_tail = h1_last[1]
            h1_final = h1_last[1]
            for tl in range(4):
                t = t0 + tl
                for half in range(2):
                    ps = p3_y1 if half == 0 else p3_y2
                    mm = None
                    for e_ in range(E):
                        for mo in range(2):
                            depm = []
                            if e_ == 0 and mo == 0:
                                depm = [gl_ops[-1], w2_ld]
                                if len(yev_hist) >= 2:
                                    depm.append(yev_hist[-2])
                            mm = O("tensor", lambda e, e_=e_, mo=mo, tl=tl,
                                   half=half, ps=ps: e.matmul(
                                       ps[:, :],
                                       mg_s[:, (e_ * 2 + mo) * 512 + tl * 128:(e_ * 2 + mo) * 512 + (tl + 1) * 128],
                                       w2_s[:, (e_ * 2 + mo) * 1024 + half * 512:(e_ * 2 + mo) * 1024 + (half + 1) * 512],
                                       start=(e_ == 0 and mo == 0), stop=False),
                                   deps=depm)
                    mm = O("tensor", lambda e, t=t, half=half, ps=ps: e.matmul(
                        ps[:, :], mask_f[0:8, (t - 1) * 128:t * 128],
                        b2_s[0:8, half * 512:(half + 1) * 512],
                        start=False, stop=True), deps=[b2_ld])
                    ev = O("vector", lambda e, tl=tl, half=half, ps=ps:
                           e.tensor_copy(
                               stg_s[:, (tl % 2) * 1024 + half * 512:(tl % 2) * 1024 + (half + 1) * 512],
                               ps[:, :]),
                           deps=[mm] + ([y_sts[-2]] if len(y_sts) >= 2
                                        else [q_sts[-1]]))
                    yev_hist.append(ev)
                    if half == 1:
                        y_sts.append(O("sync", lambda e, t=t, tl=tl: e.dma_start(
                            out=y_d[(t - 1) * 128:t * 128, :],
                            in_=stg_s[:, (tl % 2) * 1024:(tl % 2) * 1024 + 1024]),
                            deps=[ev], dma=True))
        if _stop("s3", blk, [y_sts[-1]]):
            return nc, dbg
        P.flush(blk)

    with (
        nc.psum_tensor("p5_z", [128, 64], F32) as p5_z,
        nc.psum_tensor("p5_z2", [128, 64], F32) as p5_z2,
        nc.Block("s5") as blk,
    ):
        P.serial = True
        z1_ld = O("gpsimd", lambda e: e.dma_start(
            out=z1a.ap().rearrange("p (h mo c) -> p h mo c", mo=2, c=65),
            in_=z1out_d.ap().rearrange("(h mo p) c -> p h mo c", p=128, mo=2)),
            deps=[cc_z1], dma=True)
        inv_ld = O("gpsimd", lambda e: e.dma_start(
            out=invT_s.ap().rearrange("p (h kk c) -> p h kk c", kk=2, c=256),
            in_=invout_d.ap().rearrange("(h kk p) c -> p h kk c", p=128, kk=2)),
            deps=[cc_inv, h1_final], dma=True)
        zb_ones = O("vector", lambda e: e.memset(
            zaug.ap().rearrange("p (hm c) -> p hm c", c=65)[:, :, 64:65], 1.0))
        a3_done = []
        for h in range(H):
            for kk in range(2):
                r1_ = O("vector", lambda e, h=h, kk=kk: e.reciprocal(
                    rec6_s[:, kk:kk + 1],
                    z1a[:, (h * 2 + kk) * 65 + 64:(h * 2 + kk) * 65 + 65]),
                    deps=[z1_ld])
                a3_done.append(O("vector", lambda e, h=h, kk=kk: e.tensor_scalar_mul(
                    a3v_s[:, (h * 2 + kk) * 64:(h * 2 + kk + 1) * 64],
                    z1a[:, (h * 2 + kk) * 65:(h * 2 + kk) * 65 + 64],
                    rec6_s[:, kk:kk + 1]), deps=[r1_]))
        z_evs = []
        for h in range(H):
            for mo in range(2):
                ps = p5_z if mo == 0 else p5_z2
                mm = None
                for kk in range(2):
                    depm = []
                    if kk == 0:
                        depm = [a3_done[2 * h + 1], inv_ld]
                        if len(z_evs) >= 2:
                            depm.append(z_evs[-2])
                    mm = O("tensor", lambda e, h=h, mo=mo, kk=kk, ps=ps: e.matmul(
                        ps[:, :],
                        invT_s[:, h * 512 + kk * 256 + mo * 128:h * 512 + kk * 256 + (mo + 1) * 128],
                        a3v_s[:, (h * 2 + kk) * 64:(h * 2 + kk + 1) * 64],
                        start=(kk == 0), stop=(kk == 1)), deps=depm)
                z_evs.append(O("vector", lambda e, h=h, mo=mo, ps=ps: e.tensor_copy(
                    zaug[:, (h * 2 + mo) * 65:(h * 2 + mo) * 65 + 64], ps[:, :]),
                    deps=[mm, zb_ones]))
        zaug_done = z_evs[-1]
        if dbg_t("zaug", [128, H * 2 * 65], BF16):
            O("sync", lambda e: e.dma_start(out=dbg["zaug"][:, :],
                                            in_=zaug[:, :]), deps=[zaug_done], dma=True)
        if dbg_t("z1a", [128, H * 2 * 65]):
            O("sync", lambda e: e.dma_start(out=dbg["z1a"][:, :],
                                            in_=z1a[:, :]), deps=[z1_ld], dma=True)
        if _stop("s5", blk, [zaug_done]):
            return nc, dbg
        P.flush(blk)

    with (
        nc.psum_tensor("p6_s1", [128, 1024], F32) as p6_s1,
        nc.psum_tensor("p6_s1b", [128, 1024], F32) as p6_s1b,
        nc.psum_tensor("p6_o0", [128, 65], F32) as p6_o0,
        nc.psum_tensor("p6_o1", [128, 65], F32) as p6_o1,
        nc.psum_tensor("p6_o2", [128, 65], F32) as p6_o2,
        nc.psum_tensor("p6_o3", [128, 65], F32) as p6_o3,
        nc.Block("s6") as blk,
    ):
        wfc_ld = O("gpsimd", lambda e: e.dma_start(
            out=wfc_s.ap().rearrange("p (fi c) -> p fi c", c=1024),
            in_=wfc.ap().rearrange("(fi p) c -> p fi c", p=128)),
            deps=[yev_hist[-1]], dma=True)
        pso = [p6_o0, p6_o1, p6_o2, p6_o3]
        on_last = None
        exp_par = {0: None, 1: None}
        o_readers = {0: None, 1: None, 2: None, 3: None}
        last_sim1 = None
        exp_prev = None
        for ci, (t0, ntl) in enumerate(CHUNKS):
            W = ntl * 128
            q_dep = [q_sts[-1], zaug_done, yev_hist[-1]]
            if last_sim1 is not None:
                q_dep.append(last_sim1)
            q_ld = O("scalar", lambda e, t0=t0, W=W: e.dma_start(
                out=qc_s[:, 0:8 * W].rearrange("p (fi c) -> p fi c", c=W),
                in_=q_d[:, t0 * 128:t0 * 128 + W]
                .rearrange("(fi p) c -> p fi c", p=128)), deps=q_dep, dma=True)
            for h in range(H):
                po = (h % 2) * 64
                fj = h // 2
                ps1 = p6_s1 if h % 2 == 0 else p6_s1b
                mm2 = None
                for mo in range(2):
                    dd = []
                    if mo == 0:
                        dd = [q_ld, zaug_done]
                        if exp_par.get(h % 2) is not None:
                            dd.append(exp_par[h % 2])
                        elif exp_prev is not None:
                            dd.append(exp_prev)
                    mm2 = O("tensor", lambda e, po=po, fj=fj, mo=mo, W=W, ps1=ps1:
                            e.matmul(
                        ps1[:, mo * 512:mo * 512 + W],
                        klT[po:po + 64, fj * 256 + mo * 128:fj * 256 + (mo + 1) * 128],
                        qc_s[po:po + 64, fj * W:(fj + 1) * W],
                        start=True, stop=True), deps=dd)
                last_sim1 = mm2
                ex = None
                for mo in range(2):
                    dd = [mm2]
                    if o_readers[3] is not None and mo == 0:
                        dd.append(o_readers[3])
                    ex = O("scalar", lambda e, mo=mo, W=W, ps1=ps1: e.activation(
                        exp1_s[:, mo * 512:mo * 512 + W],
                        ps1[:, mo * 512:mo * 512 + W], AF.Exp), deps=dd)
                exp_par[h % 2] = ex
                exp_prev = ex
                for tl in range(ntl):
                    t = t0 + tl
                    zi = tl % 4
                    omm = None
                    for mo in range(2):
                        dd = []
                        if mo == 0:
                            dd = [ex]
                            if o_readers[zi] is not None:
                                dd.append(o_readers[zi])
                        omm = O("tensor", lambda e, mo=mo, tl=tl, h=h, zi=zi:
                                e.matmul(
                                    pso[zi][:, :],
                                    exp1_s[:, mo * 512 + tl * 128:mo * 512 + (tl + 1) * 128],
                                    zaug[:, (h * 2 + mo) * 65:(h * 2 + mo + 1) * 65],
                                    start=(mo == 0), stop=(mo == 1)), deps=dd)
                    rc = O("vector", lambda e, zi=zi: e.reciprocal(
                        rec6_s[:, zi:zi + 1], pso[zi][:, 64:65]), deps=[omm])
                    if ntl == 1:
                        hcol = 0 if t0 == 0 else 1
                        rc = O("vector", lambda e, zi=zi, hcol=hcol: e.tensor_mul(
                            rec6_s[:, zi:zi + 1], rec6_s[:, zi:zi + 1],
                            hm_s[:, hcol:hcol + 1]), deps=[cst["hm"]])
                    on_last = O("vector", lambda e, t=t, h=h, zi=zi:
                                e.tensor_scalar_mul(
                                    on_s[:, t * 1024 + h * 64:t * 1024 + (h + 1) * 64],
                                    pso[zi][:, 0:64], rec6_s[:, zi:zi + 1]),
                                deps=[rc, h1_final])
                    o_readers[zi] = on_last
        on_done = on_last
        if dbg_t("on", [128, NT * 1024], BF16):
            O("sync", lambda e: e.dma_start(out=dbg["on"][:, :],
                                            in_=on_s[:, :]), deps=[on_done], dma=True)
        if _stop("s6", blk, [on_done]):
            return nc, dbg
        P.flush(blk)

    with (
        nc.psum_tensor("p7_ca", [64, 128], F32) as p7_ca,
        nc.psum_tensor("p7_cb", [64, 128], F32) as p7_cb,
        nc.psum_tensor("p7_f0", [128, 512], F32) as p7_f0,
        nc.psum_tensor("p7_f1", [128, 512], F32) as p7_f1,
        nc.Block("s7") as blk,
    ):
        out_dmas = []
        gp_hist = []
        fc_hist = []
        r_readers = None
        conv_readers = {0: None, 1: None}
        yb_hist = []
        for t in range(1, TLOC + 1):
            par = t % 2
            ybd = [y_sts[t - 1]]
            if len(gp_hist) >= 3:
                ybd.append(gp_hist[-3])
            yb_ld = O("scalar", lambda e, t=t, par=par: e.dma_start(
                out=yb_s[:, par * 1024:(par + 1) * 1024],
                in_=y_d[(t - 1) * 128:t * 128, :]), deps=ybd, dma=True)
            yb_hist.append(yb_ld)
            r_evs = []
            for fpair in range(8):
                h0, h1_ = 2 * fpair, 2 * fpair + 1
                for hh, (h, pc) in enumerate([(h0, p7_ca), (h1_, p7_cb)]):
                    mm = None
                    for dlt in (-1, 0, 1):
                        dd = []
                        if dlt == -1:
                            dd = [on_done, csb_ld]
                            if conv_readers[hh] is not None:
                                dd.append(conv_readers[hh])
                        mm = O("tensor", lambda e, t=t, h=h, dlt=dlt, pc=pc:
                               e.matmul(
                                   pc[:, :],
                                   on_s[:, (t + dlt) * 1024 + h * 64:(t + dlt) * 1024 + (h + 1) * 64],
                                   c_sb[:, (h * 3 + dlt + 1) * 128:(h * 3 + dlt + 2) * 128],
                                   start=(dlt == -1), stop=(dlt == 1)), deps=dd)
                    ev = O("vector", lambda e, fpair=fpair, hh=hh, par=par, pc=pc:
                           e.tensor_copy(
                               r_s[hh * 64:(hh + 1) * 64,
                                   par * 1024 + fpair * 128:par * 1024 + (fpair + 1) * 128],
                               pc[:, :]), deps=[mm])
                    conv_readers[hh] = ev
                    r_evs.append(ev)
            r_done = r_evs[-1]
            for half in range(2):
                pf = p7_f0 if half == 0 else p7_f1
                mm = None
                for fi in range(8):
                    dd = []
                    if fi == 0:
                        dd = [r_done, wfc_ld]
                        if len(fc_hist) >= 2:
                            dd.append(fc_hist[-2])
                    mm = O("tensor", lambda e, fi=fi, half=half, par=par, pf=pf:
                           e.matmul(
                               pf[:, :],
                               r_s[:, par * 1024 + fi * 128:par * 1024 + (fi + 1) * 128],
                               wfc_s[:, fi * 1024 + half * 512:fi * 1024 + (half + 1) * 512],
                               start=(fi == 0), stop=False), deps=dd)
                mm = O("tensor", lambda e, half=half, pf=pf: e.matmul(
                    pf[:, :], ones_b[0:1, :], bfc_s[0:1, half * 512:(half + 1) * 512],
                    start=False, stop=True), deps=[bfc_ld])
                vadd = O("vector", lambda e, t=t, half=half, par=par, pf=pf:
                         e.scalar_tensor_tensor(
                             outs_s[:, par * 1024 + half * 512:par * 1024 + (half + 1) * 512],
                             pf[:, :], 1.0,
                             x_bf[:, t * 1024 + half * 512:t * 1024 + (half + 1) * 512],
                             AL.mult, AL.add),
                         deps=[mm] + ([out_dmas[-2]] if len(out_dmas) >= 2 else []))
                gp_hist.append(O("vector", lambda e, half=half, par=par:
                                 e.tensor_add(
                                     outs_s[:, par * 1024 + half * 512:par * 1024 + (half + 1) * 512],
                                     outs_s[:, par * 1024 + half * 512:par * 1024 + (half + 1) * 512],
                                     yb_s[:, par * 1024 + half * 512:par * 1024 + (half + 1) * 512]),
                                 deps=[vadd, yb_ld]))
                fc_hist.append(gp_hist[-1])
            out_dmas.append(O("sync", lambda e, t=t, par=par: e.dma_start(
                out=out[(t - 1) * 128:t * 128, :],
                in_=outs_s[:, par * 1024:(par + 1) * 1024]),
                deps=gp_hist[-2:], dma=True))
        O("sync", lambda e: e.nop(), deps=out_dmas)
        P.flush(blk)

    return nc, dbg


_CACHE = {}


def _build_cached():
    if "nc" not in _CACHE:
        _CACHE["nc"] = build_program()[0]
    return _CACHE["nc"]


def _prep_inputs(kw):
    x = np.ascontiguousarray(kw["L0_attn_x"], dtype=np.float32)
    wqkv = np.ascontiguousarray(kw["L0_attn_w_qkv"], dtype=np.float32)
    wconv = np.ascontiguousarray(kw["L0_attn_w_conv"].reshape(H, KW), dtype=np.float32)
    wfc = np.ascontiguousarray(kw["L0_attn_w_fc"], dtype=np.float32)
    bfc = np.ascontiguousarray(kw["L0_attn_b_fc"].reshape(1, D), dtype=np.float32)
    wg = np.ascontiguousarray(kw["L0_ffn_Wg"], dtype=np.float32)
    lns = np.ascontiguousarray(kw["L0_ffn_ln_scale"], dtype=np.float32)
    lnb = np.ascontiguousarray(kw["L0_ffn_ln_bias"], dtype=np.float32)
    w1 = np.ascontiguousarray(kw["L0_ffn_W1"], dtype=np.float32)
    b1 = np.ascontiguousarray(kw["L0_ffn_b1"], dtype=np.float32)
    w2 = np.ascontiguousarray(kw["L0_ffn_W2"], dtype=np.float32)
    b2 = np.ascontiguousarray(kw["L0_ffn_b2"], dtype=np.float32)
    in_maps = []
    for c in range(8):
        b, qtr = c // 4, c % 4
        xi = np.zeros((NT * 128, D), dtype=np.float32)
        lo, hi = qtr * TC - 128, (qtr + 1) * TC + 128
        slo, shi = max(lo, 0), min(hi, N)
        xi[slo - lo:slo - lo + (shi - slo)] = x[b, slo:shi]
        hm = np.zeros((128, 2), dtype=np.float32)
        hm[:, 0] = 1.0 if qtr > 0 else 0.0
        hm[:, 1] = 1.0 if qtr < 3 else 0.0
        hs = np.zeros((128, 4), dtype=np.float32)
        hs[:, qtr] = 1.0
        in_maps.append(dict(
            xin=xi, hmask=hm, hsel=hs, wqkv=wqkv, wconv=wconv, wfc=wfc,
            bfc=bfc, wg=wg, lns=lns, lnb=lnb, w1=w1, b1=b1, w2=w2, b2=b2))
    return in_maps


def kernel(**kw):
    nc = _build_cached()
    in_maps = _prep_inputs(kw)
    res = run_bass_kernel_spmd(nc, in_maps, core_ids=list(range(8)))
    y = np.empty((B, N, D), dtype=np.float32)
    for c in range(8):
        b, qtr = c // 4, c % 4
        y[b, qtr * TC:(qtr + 1) * TC, :] = res.results[c]["out"]
    return y


if __name__ == "__main__":
    nc, _ = build_program()
    print("instructions:", nc.n_instructions())


# version 05 (final)
# speedup vs baseline: 1.0671x; 1.0671x over previous
import numpy as np
import ml_dtypes

import concourse.bass as bass
import concourse.mybir as mybir
from concourse.bass_utils import run_bass_kernel_spmd

F32 = mybir.dt.float32
BF16 = mybir.dt.bfloat16
AF = mybir.ActivationFunctionType
AL = mybir.AluOpType
AX = mybir.AxisListType
bfdt = ml_dtypes.bfloat16

B, N, D = 2, 8192, 1024
H, DH, M, ITERS, KW = 16, 64, 256, 6, 13
E, HF = 8, 256
LN_EPS = 1e-5
TC = 2048
NT = 18
TLOC = 16
GROUPS = [[0, 1, 2, 3], [4, 5, 6, 7]]
QSCALE = DH ** -0.5
CHUNKS = [(1, 4), (5, 4), (9, 4), (13, 4), (0, 1), (17, 1)]
ENGS = ("tensor", "vector", "scalar", "gpsimd", "sync")


class Op:
    __slots__ = ("eng", "emit", "deps", "count", "inc", "asyn", "key")

    def __init__(self, eng, emit, deps, count, inc, asyn, key):
        self.eng, self.emit, self.deps, self.count, self.inc, self.asyn, \
            self.key = eng, emit, deps, count, inc, asyn, key


SEMKEYS = [(e, k) for e in ENGS for k in ("c", "d")]


class Prog:

    def __init__(self, nc):
        self.nc = nc
        self.sem = {k: nc.alloc_semaphore(f"sem_{k[0]}_{k[1]}") for k in SEMKEYS}
        self.total = {k: 0 for k in SEMKEYS}
        self.pend = {e: [] for e in ENGS}
        self.water = {c: {k: 0 for k in SEMKEYS} for c in ENGS}
        self.serial = False
        self.last = {e: None for e in ENGS}

    def op(self, eng, emit, deps=(), dma=False, asyn=None):
        key = (eng, "d" if dma else "c")
        k = 16 if dma else 1
        self.total[key] += k
        dl = [d for d in deps if d is not None]
        if self.serial and not dma and eng in ("vector", "scalar", "gpsimd") \
                and self.last[eng] is not None:
            dl.append(self.last[eng])
        o = Op(eng, emit, dl, self.total[key],
               k, dma if asyn is None else asyn, key)
        self.pend[eng].append(o)
        self.last[eng] = o
        return o

    def flush(self, blk):
        for eng in ENGS:
            ops = self.pend[eng]
            if not ops:
                continue

            def body(e, eng=eng, ops=list(ops)):
                for o in ops:
                    for d in o.deps:
                        if d.count > self.water[eng][d.key]:
                            e.wait_ge(self.sem[d.key], d.count)
                            self.water[eng][d.key] = d.count
                    o.emit(e).then_inc(self.sem[o.key], o.inc)
            getattr(blk, eng)(body)
            self.pend[eng] = []


def _esz(dt):
    return 2 if dt == BF16 else 4


class SbufMap:
    def __init__(self, nc, base=16512, limit=229300):
        self.nc = nc
        self.off = base
        self.limit = limit

    def alloc(self, name, shape, dtype):
        self.off = (self.off + 63) // 64 * 64
        t = self.nc.alloc_sbuf_tensor_at(name, shape, dtype, offset=self.off)
        self.off += int(np.prod(shape[1:])) * _esz(dtype)
        assert self.off <= self.limit, f"SBUF overflow at {name}: {self.off}"
        return t

    def group(self, base, specs):
        off = base
        res = []
        for name, shape, dtype in specs:
            off = (off + 63) // 64 * 64
            res.append(self.nc.alloc_sbuf_tensor_at(name, shape, dtype, offset=off))
            off += int(np.prod(shape[1:])) * _esz(dtype)
        assert off <= self.limit, f"overlay overflow at {name}: {off}"
        return res, off


def _constants():
    c = {}
    c["id_f32"] = np.eye(128, dtype=np.float32)
    c["id_bf"] = np.eye(128, dtype=bfdt)
    c["ones_bf"] = np.ones((1, 128), dtype=bfdt)
    g = np.zeros((128, 4), dtype=bfdt)
    for j in range(4):
        g[j * 32:(j + 1) * 32, j] = 1.0
    c["g_bf"] = g
    Eb = np.zeros((13, 3 * 16384), dtype=bfdt)
    p_i, q_i = np.meshgrid(np.arange(128), np.arange(128), indexing="ij")
    for pat in range(3):
        kk = q_i - p_i - 128 * (pat - 1) + 6
        ok = (kk >= 0) & (kk < 13)
        Eb[kk[ok], pat * 16384 + (p_i * 128 + q_i)[ok]] = 1.0
    c["ebas"] = Eb

    def packI(s):
        m = np.zeros((128, 512), dtype=bfdt)
        for mo in range(2):
            np.fill_diagonal(m[:, mo * 256 + mo * 128: mo * 256 + mo * 128 + 128], s)
        return m
    c["i7"], c["i15"], c["i13q"] = packI(7.0), packI(15.0), packI(13.0 * 0.25)
    e6 = np.zeros((13, 16), dtype=np.float32)
    e6[6, :] = 1.0
    c["e6"] = e6
    sel8 = np.zeros((8, 8 * 128), dtype=bfdt)
    for e in range(8):
        sel8[e, e * 128:(e + 1) * 128] = 1.0
    c["sel8"] = sel8
    return c


def build_program(debug=(), stop_after=None):
    nc = bass.Bass("TRN2", target_bir_lowering=False,
                   detect_race_conditions=False)

    xin = nc.dram_tensor("xin", [NT * 128, D], F32, kind="ExternalInput")
    hmask = nc.dram_tensor("hmask", [128, 2], F32, kind="ExternalInput")
    hsel = nc.dram_tensor("hsel", [128, 4], F32, kind="ExternalInput")
    wqkv = nc.dram_tensor("wqkv", [D, 3 * D], F32, kind="ExternalInput")
    wconv = nc.dram_tensor("wconv", [H, KW], F32, kind="ExternalInput")
    wfc = nc.dram_tensor("wfc", [D, D], F32, kind="ExternalInput")
    bfc = nc.dram_tensor("bfc", [1, D], F32, kind="ExternalInput")
    wg = nc.dram_tensor("wg", [E, D], F32, kind="ExternalInput")
    lns = nc.dram_tensor("lns", [E, D], F32, kind="ExternalInput")
    lnb = nc.dram_tensor("lnb", [E, D], F32, kind="ExternalInput")
    w1 = nc.dram_tensor("w1", [E, D, HF], F32, kind="ExternalInput")
    b1 = nc.dram_tensor("b1", [E, HF], F32, kind="ExternalInput")
    w2 = nc.dram_tensor("w2", [E, HF, D], F32, kind="ExternalInput")
    b2 = nc.dram_tensor("b2", [E, D], F32, kind="ExternalInput")
    out = nc.dram_tensor("out", [TC, D], F32, kind="ExternalOutput")

    dbg = {}

    def dbg_t(name, shape, dtype=F32):
        if name in debug:
            dbg[name] = nc.dram_tensor("dbg_" + name, list(shape), dtype,
                                       kind="ExternalOutput")
        return dbg.get(name)

    q_d = nc.dram_tensor("q_d", [D, NT * 128], BF16)
    xh_d = nc.dram_tensor("xh_d", [D, TC], BF16)
    y_d = nc.dram_tensor("y_d", [TC, D], BF16)
    w1p_d = nc.dram_tensor("w1p_d", [E * D, HF], BF16)
    cflat_d = nc.dram_tensor("cflat_d", [16, 3 * 16384], BF16)
    lmin_d = nc.dram_tensor("lmin_d", [64, 2048], BF16)
    lmout_d = nc.dram_tensor("lmout_d", [256, 2048], BF16)
    z1in_d = nc.dram_tensor("z1in_d", [H * 256, 65], F32)
    z1out_d = nc.dram_tensor("z1out_d", [H * 256, 65], F32)
    invin_d = nc.dram_tensor("invin_d", [4 * 256, 256], BF16)
    invout_d = nc.dram_tensor("invout_d", [H * 256, 256], BF16)

    Cs = _constants()
    id_f32_c = nc.inline_tensor(Cs["id_f32"], name="id_f32_c")
    id_bf_c = nc.inline_tensor(np.asarray(Cs["id_bf"]), name="id_bf_c")
    ones_bf_c = nc.inline_tensor(np.asarray(Cs["ones_bf"]), name="ones_bf_c")
    g_bf_c = nc.inline_tensor(np.asarray(Cs["g_bf"]), name="g_bf_c")
    ebas_c = nc.inline_tensor(np.asarray(Cs["ebas"]), name="ebas_c")
    i7_c = nc.inline_tensor(np.asarray(Cs["i7"]), name="i7_c")
    i15_c = nc.inline_tensor(np.asarray(Cs["i15"]), name="i15_c")
    i13q_c = nc.inline_tensor(np.asarray(Cs["i13q"]), name="i13q_c")
    sel8_c = nc.inline_tensor(np.asarray(Cs["sel8"]), name="sel8_c")
    e6_c = nc.inline_tensor(Cs["e6"], name="e6_c")

    sm = SbufMap(nc)
    x_bf = sm.alloc("x_bf", [128, NT * 1024], BF16)
    c_sb = sm.alloc("c_sb", [128, H * 3 * 128], BF16)
    z1a = sm.alloc("z1a", [128, H * 2 * 65], F32)
    qlT = sm.alloc("qlT", [128, 8 * 256], BF16)
    klT = sm.alloc("klT", [128, 8 * 256], BF16)
    zaug = sm.alloc("zaug", [128, H * 2 * 65], BF16)
    id_f = sm.alloc("id_f", [128, 128], F32)
    id_b = sm.alloc("id_b", [128, 128], BF16)
    ones_b = sm.alloc("ones_b", [1, 128], BF16)
    onec_b = sm.alloc("onec_b", [128, 1], BF16)
    g_b = sm.alloc("g_b", [128, 4], BF16)
    i7_s = sm.alloc("i7_s", [128, 512], BF16)
    i15_s = sm.alloc("i15_s", [128, 512], BF16)
    i13_s = sm.alloc("i13_s", [128, 512], BF16)
    hm_s = sm.alloc("hm_s", [128, 2], F32)
    hsel_s = sm.alloc("hsel_s", [128, 4], F32)
    stat = sm.alloc("stat", [128, 13 * NT], F32)
    cols2 = sm.alloc("cols2", [128, 2 * NT], F32)
    rows_b = sm.alloc("rows_b", [2 * NT, 128], BF16)
    rows1_b = sm.alloc("rows1_b", [1, 2 * NT * 128], BF16)
    sel8_s = sm.alloc("sel8_s", [8, 8 * 128], BF16)
    dpair = sm.alloc("dpair", [128, 256], BF16)
    gw_s = sm.alloc("gw_s", [128, 4 * TLOC], BF16)
    nmc_b = sm.alloc("nmc_b", [128, TLOC], BF16)
    mrow_b = sm.alloc("mrow_b", [1, 4 * TLOC], BF16)
    wgT_s = sm.alloc("wgT_s", [128, 64], BF16)
    mask_f = sm.alloc("mask_f", [8, TLOC * 128], BF16)
    b2_s = sm.alloc("b2_s", [8, D], BF16)
    bfc_s = sm.alloc("bfc_s", [1, D], BF16)
    cb_s = sm.alloc("cb_s", [128, 16], F32)
    lns_s = sm.alloc("lns_s", [128, 64], F32)
    lnb_s = sm.alloc("lnb_s", [128, 64], F32)
    b1_s = sm.alloc("b1_s", [128, 16], F32)
    wc_s = sm.alloc("wc_s", [16, 16], F32)
    waug_f = sm.alloc("waug_f", [13, 16], F32)
    e6_s = sm.alloc("e6_s", [13, 16], F32)
    waug_b = sm.alloc("waug_b", [13, 16], BF16)
    lgt_s = sm.alloc("lgt_s", [8, 512], F32)
    lgtT_s = sm.alloc("lgtT_s", [128, 10], F32)
    mtm_s = sm.alloc("mtm_s", [128, 8], F32)
    xnlm_s = sm.alloc("xnlm_s", [128, 512], BF16)
    lmst_s = sm.alloc("lmst_s", [128, 1024], BF16)
    nrm_s = sm.alloc("nrm_s", [128, 4], F32)
    nrow_s = sm.alloc("nrow_s", [1, 260], F32)
    nrow2_s = sm.alloc("nrow2_s", [1, 260], F32)
    d2c = sm.alloc("d2c", [128, 32], F32)
    stg_s = sm.alloc("stg_s", [128, 2 * 1024], BF16)
    rcc_b = sm.alloc("rcc_b", [1, 4], BF16)
    rec6_s = sm.alloc("rec6_s", [128, 8], F32)
    zro_s = sm.alloc("zro_s", [128, 512], BF16)

    base0 = sm.off
    (wqkv_s,), e1 = sm.group(base0, [("wqkv_s", [128, 8 * 3072], BF16)])
    (w2_s, mg_s), e2 = sm.group(base0, [("w2_s", [128, 16 * 1024], BF16),
                                        ("mg_s", [128, 16 * 512], BF16)])
    (wfc_s, qc_s, exp1_s, yb_s, r_s, outs_s), e3 = sm.group(
        base0, [("wfc_s", [128, 8 * 1024], BF16),
                ("qc_s", [128, 2 * 8 * 512], BF16),
                ("exp1_s", [128, 2 * 512], BF16),
                ("yb_s", [128, 2 * 1024], BF16),
                ("r_s", [128, 2 * 1024], BF16),
                ("outs_s", [128, 2 * 1024], F32)])
    base1 = max(e1, e2, e3)
    (w1_s, w1p_s0), f1 = sm.group(base1, [("w1_s", [128, 2 * 2048], F32),
                                          ("w1p_s0", [128, 2 * 2048], BF16)])
    (wg_s, lns_r, lnb_r, b1_r), f6 = sm.group(
        base1, [("wg_s", [8, 1024], F32), ("lns_r", [8, 1024], F32),
                ("lnb_r", [8, 1024], F32), ("b1_r", [8, 256], F32)])
    (e2all, a_nw, e2_nw, s_nw, z_nw, y_nw, x1_nw, x2_nw, pt_nw), f4 = sm.group(
        base1, [("e2all", [128, H * 512], BF16),
                ("a_nw", [128, 512], BF16),
                ("e2_nw", [128, 512], F32),
                ("s_nw", [128, 512], BF16),
                ("z_nw", [128, 512], BF16),
                ("y_nw", [128, 512], BF16),
                ("x1_nw", [128, 512], BF16),
                ("x2_nw", [128, 512], BF16),
                ("pt_nw", [128, 512], BF16)])
    (xn_s, xh_s, k_s, v_s, exp3_s), f2 = sm.group(
        base1, [("xn_s", [128, 8 * 512], BF16),
                ("xh_s", [128, 8 * 512], BF16),
                ("k_s", [128, 8 * 512], BF16),
                ("v_s", [128, 4 * 1040], BF16),
                ("exp3_s", [128, 2 * 512], BF16)])
    w1p_s = xn_s
    (invT_s, a3v_s), f5 = sm.group(
        base1, [("invT_s", [128, H * 2 * 256], BF16),
                ("a3v_s", [128, H * 2 * 64], BF16)])
    (on_s,), f3 = sm.group(base1, [("on_s", [128, NT * 1024], BF16)])
    (lmag_s, lmg_s, ebas_s, cfb_s), gg1 = sm.group(
        f1, [("lmag_s", [64, 2048], BF16),
             ("lmg_s", [128, 2048], BF16),
             ("ebas_s", [13, 2048], BF16),
             ("cfb_s", [16, 2048], BF16)])
    assert max(f2, f3, f4, f5, f6, gg1) <= 229300, \
        (f2, f3, f4, f5, f6, gg1)

    (IX_SX, IX_SQ, IX_MU, IX_VAR, IX_R1, IX_R2, IX_R12, IX_M1R1, IX_M1R12,
     IX_R132, IX_NM132, IX_T0, IX_T1) = [i * NT for i in range(13)]

    def stc(ix, t):
        return stat[:, ix + t:ix + t + 1]

    P = Prog(nc)
    P.serial = True
    O = P.op

    def _stop(tag, blk, dep_ops):
        if stop_after != tag:
            return False
        d = O("sync", lambda e: e.dma_start(out=out[:, :], in_=xin[0:TC, :]),
              deps=dep_ops, dma=True)
        O("sync", lambda e: e.nop(), deps=[d])
        P.flush(blk)
        return True

    with (
        nc.psum_tensor("p0_d1", [128, 512], F32) as p0_d1,
        nc.psum_tensor("p0_d2", [128, 512], F32) as p0_d2,
        nc.psum_tensor("p0_d3", [128, 512], F32) as p0_d3,
        nc.psum_tensor("p0_d4", [128, 512], F32) as p0_d4,
        nc.Block("s0a") as blk,
    ):
        cst = {}
        for nm, dst, src in [("idf", id_f, id_f32_c), ("idb", id_b, id_bf_c),
                             ("ones", ones_b, ones_bf_c), ("g", g_b, g_bf_c),
                             ("i7", i7_s, i7_c), ("i15", i15_s, i15_c),
                             ("i13", i13_s, i13q_c), ("hm", hm_s, hmask),
                             ("hsel", hsel_s, hsel)]:
            cst[nm] = O("sync", lambda e, d=dst, s2=src: e.dma_start(
                out=d[:, :], in_=s2[:, :]), dma=True)
        cst["wc"] = O("sync", lambda e: e.dma_start(
            out=wc_s[0:16, 0:13], in_=wconv[:, :]), dma=True)
        cst["sel8"] = O("sync", lambda e: e.dma_start(
            out=sel8_s[0:8, :], in_=sel8_c[:, :]), dma=True)
        cst["e6"] = O("sync", lambda e: e.dma_start(
            out=e6_s[0:13, :], in_=e6_c[:, :]), dma=True)
        cst["lns"] = O("sync", lambda e: e.dma_start(
            out=lns_r[0:8, :], in_=lns[:, :]), dma=True)
        cst["lnb"] = O("sync", lambda e: e.dma_start(
            out=lnb_r[0:8, :], in_=lnb[:, :]), dma=True)
        cst["b1"] = O("sync", lambda e: e.dma_start(
            out=b1_r[0:8, :], in_=b1[:, :]), dma=True)

        xbf_ld = O("gpsimd", lambda e: e.dma_start(
            out=x_bf.ap().rearrange("p (t f) -> p t f", f=1024),
            in_=xin.ap().rearrange("(t p) f -> p t f", p=128)), dma=True)
        wqkv_ld = O("gpsimd", lambda e: e.dma_start(
            out=wqkv_s.ap().rearrange("p (fi c) -> p fi c", c=3072),
            in_=wqkv.ap().rearrange("(fi p) c -> p fi c", p=128)), dma=True)
        b2_ld = O("gpsimd", lambda e: e.dma_start(out=b2_s[:, :], in_=b2[:, :]),
                  dma=True)
        bfc_ld = O("gpsimd", lambda e: e.dma_start(out=bfc_s[:, :], in_=bfc[:, :]),
                   dma=True)
        wgT_ld = O("gpsimd", lambda e: e.dma_start(
            out=wg_s[0:8, :], in_=wg[:, :]), dma=True)

        zro0 = O("vector", lambda e: e.memset(zro_s[:, :], 0.0))
        sq_acts = []
        if dbg_t("xbf", [128, NT * 1024], BF16):
            O("sync", lambda e: e.dma_start(out=dbg["xbf"][:, :],
                                            in_=x_bf[:, :]), deps=[xbf_ld], dma=True)
        sq_last = None
        sqr_ops = []
        for t in range(NT):
            for hf in range(2):
                i = 2 * t + hf
                ps = p0_d1 if i % 2 == 0 else p0_d2
                dd = [xbf_ld]
                if len(sqr_ops) >= 2:
                    dd.append(sqr_ops[-2])
                ac = O("scalar", lambda e, t=t, hf=hf, ps=ps: e.activation(
                    ps[:, :], x_bf[:, t * 1024 + hf * 512:t * 1024 + (hf + 1) * 512],
                    AF.Square), deps=dd)
                sqr_ops.append(O("vector", lambda e, t=t, hf=hf, ps=ps: e.reduce_sum(
                    stc(IX_T0 + hf * NT, t), ps[:, :], axis=AX.X), deps=[ac]))
        sq_last = sqr_ops[-1]
        for t in range(NT):
            O("vector", lambda e, t=t: e.reduce_sum(
                stc(IX_SX, t), x_bf[:, t * 1024:t * 1024 + 512], axis=AX.X),
                deps=[xbf_ld])
            O("vector", lambda e, t=t: e.reduce_sum(
                stc(IX_SQ, t), x_bf[:, t * 1024 + 512:(t + 1) * 1024], axis=AX.X))
        onec = O("vector", lambda e: e.memset(onec_b[:, :], 1.0))
        O("vector", lambda e: e.tensor_add(
            stat[:, IX_SX:IX_SX + NT], stat[:, IX_SX:IX_SX + NT],
            stat[:, IX_SQ:IX_SQ + NT]))
        O("vector", lambda e: e.tensor_add(
            stat[:, IX_SQ:IX_SQ + NT], stat[:, IX_T0:IX_T0 + NT],
            stat[:, IX_T1:IX_T1 + NT]), deps=[sq_last])
        O("vector", lambda e: e.tensor_scalar_mul(
            stat[:, IX_MU:IX_MU + NT], stat[:, IX_SX:IX_SX + NT], 1.0 / D))
        O("vector", lambda e: e.tensor_mul(
            stat[:, IX_T0:IX_T0 + NT], stat[:, IX_MU:IX_MU + NT],
            stat[:, IX_MU:IX_MU + NT]))
        O("vector", lambda e: e.tensor_scalar_mul(
            stat[:, IX_T1:IX_T1 + NT], stat[:, IX_SQ:IX_SQ + NT], 1.0 / D))
        O("vector", lambda e: e.tensor_sub(
            stat[:, IX_VAR:IX_VAR + NT], stat[:, IX_T1:IX_T1 + NT],
            stat[:, IX_T0:IX_T0 + NT]))
        O("vector", lambda e: e.tensor_scalar_add(
            stat[:, IX_T0:IX_T0 + NT], stat[:, IX_VAR:IX_VAR + NT], LN_EPS))
        O("vector", lambda e: e.reciprocal(
            stat[:, IX_T1:IX_T1 + NT], stat[:, IX_T0:IX_T0 + NT]))
        O("vector", lambda e: e.tensor_mul(
            stat[:, IX_T0:IX_T0 + NT], stat[:, IX_VAR:IX_VAR + NT],
            stat[:, IX_T1:IX_T1 + NT]))
        O("vector", lambda e: e.tensor_scalar_add(
            stat[:, IX_T0:IX_T0 + NT], stat[:, IX_T0:IX_T0 + NT], LN_EPS))
        recs = O("vector", lambda e: e.reciprocal(
            stat[:, IX_T0:IX_T0 + NT], stat[:, IX_T0:IX_T0 + NT]))
        O("scalar", lambda e: e.activation(
            stat[:, IX_R1:IX_R1 + NT], stat[:, IX_T1:IX_T1 + NT], AF.Sqrt),
            deps=[recs])
        sq2 = O("scalar", lambda e: e.activation(
            stat[:, IX_R2:IX_R2 + NT], stat[:, IX_T0:IX_T0 + NT], AF.Sqrt))
        O("vector", lambda e: e.tensor_mul(
            stat[:, IX_R12:IX_R12 + NT], stat[:, IX_R1:IX_R1 + NT],
            stat[:, IX_R2:IX_R2 + NT]), deps=[sq2])
        O("vector", lambda e: e.scalar_tensor_tensor(
            stat[:, IX_M1R1:IX_M1R1 + NT], stat[:, IX_MU:IX_MU + NT], -1.0,
            stat[:, IX_R1:IX_R1 + NT], AL.mult, AL.mult))
        O("vector", lambda e: e.tensor_mul(
            stat[:, IX_M1R12:IX_M1R12 + NT], stat[:, IX_M1R1:IX_M1R1 + NT],
            stat[:, IX_R2:IX_R2 + NT]))
        O("vector", lambda e: e.tensor_scalar_mul(
            stat[:, IX_R132:IX_R132 + NT], stat[:, IX_R1:IX_R1 + NT], 1.0 / 32))
        O("vector", lambda e: e.tensor_scalar_mul(
            stat[:, IX_NM132:IX_NM132 + NT], stat[:, IX_M1R1:IX_M1R1 + NT],
            1.0 / 32))
        for t in range(NT):
            O("vector", lambda e, t=t: e.tensor_copy(
                cols2[:, 2 * t:2 * t + 1], stc(IX_M1R1, t)))
            O("vector", lambda e, t=t: e.tensor_copy(
                cols2[:, 2 * t + 1:2 * t + 2], stc(IX_M1R12, t)))
        gw_ops = []
        for tl in range(TLOC):
            O("vector", lambda e, tl=tl: e.tensor_scalar_mul(
                gw_s[:, 4 * tl:4 * tl + 4], g_b[:, :], stc(IX_R132, tl + 1)),
                deps=[cst["g"]])
            gw_ops.append(O("vector", lambda e, tl=tl: e.tensor_copy(
                nmc_b[:, tl:tl + 1], stc(IX_NM132, tl + 1))))
        stats_done = O("vector", lambda e: e.tensor_copy(
            lgtT_s[:, 1:2], stc(IX_R1, 0)))
        if dbg_t("stat", [128, 13 * NT]):
            O("sync", lambda e: e.dma_start(out=dbg["stat"][:, :],
                                            in_=stat[:, :]),
              deps=[stats_done, sq_last], dma=True)
        if _stop("s0a", blk, [stats_done, sq_last, cst["e6"]]):
            return nc, dbg
        P.flush(blk)

    with (
        nc.psum_tensor("pb_t1", [128, 128], F32) as pb_t1,
        nc.psum_tensor("pb_t2", [128, 128], F32) as pb_t2,
        nc.psum_tensor("pb_mr", [1, 4 * TLOC], F32) as pb_mr,
        nc.psum_tensor("pb_b1", [128, 128], BF16) as pb_b1,
        nc.psum_tensor("pb_b2", [128, 128], BF16) as pb_b2,
        nc.psum_tensor("pb_lm", [128, 512], F32) as pb_lm,
        nc.psum_tensor("pb_lq", [128, 512], F32) as pb_lq,
        nc.psum_tensor("pb_lk", [128, 512], F32) as pb_lk,
        nc.Block("s0b") as blk,
    ):
        tr_rows = O("tensor", lambda e: e.transpose(
            pb_t1[0:2 * NT, :], cols2[:, :], id_f[:, :]),
            deps=[stats_done, cst["idf"]])
        mr_mm = None
        for tl in range(TLOC):
            mr_mm = O("tensor", lambda e, tl=tl: e.matmul(
                pb_mr[0:1, 4 * tl:4 * tl + 4], nmc_b[:, tl:tl + 1], g_b[:, :],
                start=True, stop=True), deps=[gw_ops[tl]])
        rows_ev0 = O("vector", lambda e: e.tensor_copy(
            rows_b[:, :], pb_t1[0:2 * NT, :]), deps=[tr_rows])
        rows_ev = O("gpsimd", lambda e: e.dma_start(
            out=rows1_b[0:1, :], in_=rows_b[:, :]), deps=[rows_ev0], dma=True)
        mrow_ev = O("vector", lambda e: e.tensor_copy(
            mrow_b[:, :], pb_mr[0:1, :]), deps=[mr_mm])

        lm_mm = None
        for tl in range(TLOC):
            tt = tl + 1
            for fi in range(8):
                O("tensor", lambda e, tt=tt, fi=fi, tl=tl: e.matmul(
                    pb_lm[:, fi * 64 + 4 * tl:fi * 64 + 4 * tl + 4],
                    x_bf[:, tt * 1024 + fi * 128:tt * 1024 + (fi + 1) * 128],
                    gw_s[:, 4 * tl:4 * tl + 4], start=True, stop=False),
                    deps=[mrow_ev, rows_ev])
                lm_mm = O("tensor", lambda e, fi=fi, tl=tl: e.matmul(
                    pb_lm[:, fi * 64 + 4 * tl:fi * 64 + 4 * tl + 4],
                    ones_b[0:1, :], mrow_b[0:1, 4 * tl:4 * tl + 4],
                    start=False, stop=True))
        xnlm_ev = O("vector", lambda e: e.tensor_copy(xnlm_s[:, :], pb_lm[:, :]),
                    deps=[lm_mm])

        lmp_mm = None
        for m in range(8):
            for fi in range(8):
                O("tensor", lambda e, m=m, fi=fi: e.matmul(
                    pb_lq[:, m * 64:(m + 1) * 64],
                    wqkv_s[:, fi * 3072 + m * 128:fi * 3072 + (m + 1) * 128],
                    xnlm_s[:, fi * 64:(fi + 1) * 64],
                    start=(fi == 0), stop=(fi == 7)), deps=[xnlm_ev, wqkv_ld])
                lmp_mm = O("tensor", lambda e, m=m, fi=fi: e.matmul(
                    pb_lk[:, m * 64:(m + 1) * 64],
                    wqkv_s[:, fi * 3072 + 1024 + m * 128:fi * 3072 + 1024 + (m + 1) * 128],
                    xnlm_s[:, fi * 64:(fi + 1) * 64],
                    start=(fi == 0), stop=(fi == 7)))
        lme_q = O("scalar", lambda e: e.activation(
            lmst_s[:, 0:512], pb_lq[:, :], AF.Copy, scale=QSCALE), deps=[lmp_mm])
        lme_k = O("scalar", lambda e: e.activation(
            lmst_s[:, 512:1024], pb_lk[:, :], AF.Copy))

        lmag_ev = []
        for m in range(16):
            ps = pb_b1 if m % 2 == 0 else pb_b2
            dd = [lme_k, cst["idb"]] if m < 2 else [lmag_ev[m - 2]]
            tr = O("tensor", lambda e, m=m, ps=ps: e.transpose(
                ps[0:64, :], lmst_s[:, m * 64:(m + 1) * 64], id_b[:, :]), deps=dd)
            lmag_ev.append(O("vector", lambda e, m=m, ps=ps: e.tensor_copy(
                lmag_s[:, m * 128:(m + 1) * 128], ps[0:64, :]), deps=[tr]))
        lmin_st = O("sync", lambda e: e.dma_start(
            out=lmin_d[:, :], in_=lmag_s[:, :]), deps=[lmag_ev[-1]], dma=True)
        cc_lm = O("gpsimd", lambda e: e.collective_compute(
            "AllGather", AL.bypass, replica_groups=GROUPS,
            ins=[lmin_d.ap().opt()], outs=[lmout_d.ap().opt()]), deps=[lmin_st])
        lmb_ev = []
        lmg_lds = []
        last_tr = None
        for g2 in range(2):
            dd = [cc_lm] if g2 == 0 else [last_tr]
            lmg_lds.append(O("gpsimd", lambda e, g2=g2: e.dma_start(
                out=lmg_s[:, :],
                in_=lmout_d[g2 * 128:(g2 + 1) * 128, :]), deps=dd, dma=True))
            for fi in range(16):
                i = g2 * 16 + fi
                ps = pb_b1 if i % 2 == 0 else pb_b2
                dd = [lmg_lds[g2]] + ([lmb_ev[i - 2]] if i >= 2 else [])
                last_tr = O("tensor", lambda e, fi=fi, ps=ps: e.transpose(
                    ps[:, :], lmg_s[:, fi * 128:(fi + 1) * 128],
                    id_b[:, :]), deps=dd)
                dst = qlT if fi < 8 else klT
                fj = fi % 8
                lmb_ev.append(O("vector", lambda e, dst=dst, fj=fj, g2=g2, ps=ps:
                                e.tensor_copy(
                                    dst[:, fj * 256 + g2 * 128:fj * 256 + (g2 + 1) * 128],
                                    ps[:, :]), deps=[last_tr]))
        lm_done = lmb_ev[-1]
        if dbg_t("qlk", [128, 2 * 2048], BF16):
            O("sync", lambda e: e.dma_start(out=dbg["qlk"][:, 0:2048],
                                            in_=qlT[:, :]), deps=[lm_done], dma=True)
            O("sync", lambda e: e.dma_start(out=dbg["qlk"][:, 2048:4096],
                                            in_=klT[:, :]), deps=[lm_done], dma=True)
        wgt_evs = []
        for fi in range(8):
            ps = pb_t1 if fi % 2 == 0 else pb_t2
            dd = [wgT_ld, cst["idf"], lm_done] if fi < 2 else [wgt_evs[fi - 2]]
            tr = O("tensor", lambda e, fi=fi, ps=ps: e.transpose(
                ps[0:128, 0:8], wg_s[0:8, fi * 128:(fi + 1) * 128],
                id_f[0:8, 0:8]), deps=dd)
            wgt_evs.append(O("vector", lambda e, fi=fi, ps=ps: e.tensor_copy(
                wgT_s[:, fi * 8:(fi + 1) * 8], ps[0:128, 0:8]), deps=[tr]))
        wgT_done = wgt_evs[-1]
        vecT_evs = []
        vt_jobs = [(lns_r, lns_s, 8), (lnb_r, lnb_s, 8), (b1_r, b1_s, 2)]
        i = 0
        for srcT, dstT, nfi in vt_jobs:
            for fi in range(nfi):
                ps = pb_t1 if i % 2 == 0 else pb_t2
                dd = [cst["lns"], cst["lnb"], cst["b1"], wgT_done] if i < 2 \
                    else [vecT_evs[i - 2]]
                tr = O("tensor", lambda e, srcT=srcT, fi=fi, ps=ps: e.transpose(
                    ps[0:128, 0:8], srcT[0:8, fi * 128:(fi + 1) * 128],
                    id_f[0:8, 0:8]), deps=dd)
                vecT_evs.append(O("vector", lambda e, dstT=dstT, fi=fi, ps=ps:
                                  e.tensor_copy(dstT[:, fi * 8:(fi + 1) * 8],
                                                ps[0:128, 0:8]), deps=[tr]))
                i += 1
        vecs_done = vecT_evs[-1]
        if _stop("s0b", blk, [vecs_done, lm_done, cc_lm]):
            return nc, dbg
        P.flush(blk)

    with (
        nc.psum_tensor("pc_ce", [128, 2], F32) as pc_ce,
        nc.psum_tensor("pc_tr", [128, 128], F32) as pc_tr,
        nc.psum_tensor("pc_c", [16, 2048], F32) as pc_c,
        nc.Block("s0c") as blk,
    ):
        w1p_st = []
        ce_ev_prev = None
        w1p_scale_last = None
        for e_ in range(E):
            dd = [cst["b1"], vecs_done, wgT_done] if e_ < 2 else [w1p_st[e_ - 2]]
            w1_ld = O("scalar", lambda e, e_=e_: e.dma_start(
                out=w1_s[:, (e_ % 2) * 2048:(e_ % 2) * 2048 + 2048]
                .rearrange("p (fi c) -> p fi c", c=256),
                in_=w1[e_, :, :].rearrange("(fi p) c -> p fi c", p=128)),
                deps=dd, dma=True)
            ce_mm = None
            base = (e_ % 2) * 2048
            for mo in range(2):
                for fi in range(8):
                    dd2 = [w1_ld, vecs_done]
                    if ce_ev_prev is not None:
                        dd2.append(ce_ev_prev)
                    ce_mm = O("tensor", lambda e, base=base, mo=mo, fi=fi, e_=e_:
                              e.matmul(pc_ce[:, mo:mo + 1],
                                       w1_s[:, base + fi * 256 + mo * 128:base + fi * 256 + (mo + 1) * 128],
                                       lnb_s[:, fi * 8 + e_:fi * 8 + e_ + 1],
                                       start=(fi == 0), stop=(fi == 7)),
                              deps=dd2 if (mo == 0 and fi == 0) else [])
            O("vector", lambda e, e_=e_: e.tensor_add(
                cb_s[:, 2 * e_:2 * e_ + 1], pc_ce[:, 0:1],
                b1_s[:, e_:e_ + 1]), deps=[ce_mm])
            ce_ev_prev = O("vector", lambda e, e_=e_: e.tensor_add(
                cb_s[:, 2 * e_ + 1:2 * e_ + 2], pc_ce[:, 1:2],
                b1_s[:, 8 + e_:9 + e_]))
            last_sc = None
            for fi in range(8):
                last_sc = O("vector", lambda e, base=base, fi=fi, e_=e_:
                            e.tensor_scalar_mul(
                                w1p_s0[:, base + fi * 256:base + (fi + 1) * 256],
                                w1_s[:, base + fi * 256:base + (fi + 1) * 256],
                                lns_s[:, fi * 8 + e_:fi * 8 + e_ + 1]),
                            deps=[w1_ld, vecs_done, ce_mm])
            w1p_scale_last = last_sc
            w1p_st.append(O("sync", lambda e, e_=e_, base=base: e.dma_start(
                out=w1p_d[e_ * 1024:(e_ + 1) * 1024, :]
                .rearrange("(fi p) c -> p fi c", p=128),
                in_=w1p_s0[:, base:base + 2048]
                .rearrange("p (fi c) -> p fi c", c=256)),
                deps=[last_sc], dma=True))

        wc_tr = O("tensor", lambda e: e.transpose(
            pc_tr[0:16, 0:16], wc_s[0:16, 0:16], id_f[0:16, 0:16]),
            deps=[cst["wc"], cst["idf"]])
        O("vector", lambda e: e.tensor_copy(waug_f[0:13, 0:16], pc_tr[0:13, 0:16]),
          deps=[wc_tr])
        O("vector", lambda e: e.tensor_add(
            waug_f[0:13, 0:16], waug_f[0:13, 0:16], e6_s[0:13, 0:16]),
          deps=[cst["e6"]])
        waug_ev = O("vector", lambda e: e.tensor_copy(
            waug_b[0:13, 0:16], waug_f[0:13, 0:16]))

        cf_st = []
        eb_ld = []
        ch_mm = []
        cb_ev = []
        for ch in range(24):
            dd = [] if ch < 1 else [ch_mm[ch - 1]]
            eb_ld.append(O("scalar", lambda e, ch=ch: e.dma_start(
                out=ebas_s[0:13, :],
                in_=ebas_c[0:13, ch * 2048:(ch + 1) * 2048]), deps=dd, dma=True))
            mm = None
            for jj in range(4):
                ddd = []
                if jj == 0:
                    ddd = [eb_ld[ch], waug_ev]
                    if ch >= 1:
                        ddd.append(cb_ev[ch - 1])
                mm = O("tensor", lambda e, ch=ch, jj=jj: e.matmul(
                    pc_c[0:16, jj * 512:(jj + 1) * 512],
                    waug_b[0:13, 0:16],
                    ebas_s[0:13, jj * 512:(jj + 1) * 512],
                    start=True, stop=True), deps=ddd)
            ch_mm.append(mm)
            cb_ev.append(O("vector", lambda e: e.tensor_copy(
                cfb_s[0:16, :], pc_c[0:16, :]), deps=[mm]))
            cf_st.append(O("gpsimd", lambda e, ch=ch: e.dma_start(
                out=cflat_d[0:16, ch * 2048:(ch + 1) * 2048], in_=cfb_s[0:16, :]),
                deps=[cb_ev[-1]] + ([cf_st[-1]] if cf_st else []), dma=True))
        csb_ld = O("gpsimd", lambda e: e.dma_start(
            out=c_sb.ap().rearrange("p (h pat q) -> p h pat q", pat=3, q=128),
            in_=cflat_d.ap().rearrange("h (pat p q) -> p h pat q",
                                       pat=3, p=128)),
            deps=[cf_st[-1]], dma=True)
        if _stop("s0c", blk, [csb_ld, w1p_st[-1], ce_ev_prev]):
            return nc, dbg
        P.flush(blk)

    with (
        nc.psum_tensor("pn_a", [128, 512], F32) as pn_a,
        nc.psum_tensor("pn_b", [128, 512], F32) as pn_b,
        nc.psum_tensor("pn_y1", [128, 256], F32) as pn_y1,
        nc.psum_tensor("pn_y2", [128, 256], F32) as pn_y2,
        nc.psum_tensor("pn_cs", [128, 2], F32) as pn_cs,
        nc.psum_tensor("pn_t", [128, 128], BF16) as pn_t,
        nc.psum_tensor("pn_rc", [128, 1], F32) as pn_rc,
        nc.Block("sn") as blk,
    ):
        e2_evs = []
        for h in range(H):
            po = (h % 2) * 64
            fj = h // 2
            ps = pn_a if h % 2 == 0 else pn_b
            mm = None
            for mo in range(2):
                dd = []
                if mo == 0:
                    dd = [lm_done] if h < 2 else [e2_evs[h - 2]]
                mm = O("tensor", lambda e, po=po, fj=fj, mo=mo, ps=ps: e.matmul(
                    ps[:, mo * 256:(mo + 1) * 256],
                    qlT[po:po + 64, fj * 256 + mo * 128:fj * 256 + (mo + 1) * 128],
                    klT[po:po + 64, fj * 256:(fj + 1) * 256],
                    start=True, stop=True), deps=dd)
            ex1d = [mm] + ([e2_evs[h - 1]] if h >= 1 else
                           [w1p_st[-1], ce_ev_prev])
            O("scalar", lambda e, h=h, ps=ps: e.activation(
                e2_nw[:, 0:256], ps[:, 0:256], AF.Exp), deps=ex1d)
            ex2 = O("scalar", lambda e, h=h, ps=ps: e.activation(
                e2_nw[:, 256:512], ps[:, 256:512], AF.Exp))
            O("vector", lambda e, h=h: e.reduce_sum(
                d2c[:, 2 * h:2 * h + 1], e2_nw[:, 0:256], axis=AX.X), deps=[ex2])
            O("vector", lambda e, h=h: e.reduce_sum(
                d2c[:, 2 * h + 1:2 * h + 2], e2_nw[:, 256:512], axis=AX.X))
            O("vector", lambda e, h=h: e.tensor_add(
                d2c[:, 2 * h:2 * h + 1], d2c[:, 2 * h:2 * h + 1],
                d2c[:, 2 * h + 1:2 * h + 2]))
            O("vector", lambda e, h=h: e.reciprocal(
                d2c[:, 2 * h + 1:2 * h + 2], d2c[:, 2 * h:2 * h + 1]))
            e2_evs.append(O("vector", lambda e, h=h: e.tensor_scalar_mul(
                e2all[:, h * 512:(h + 1) * 512], e2_nw[:, :],
                d2c[:, 2 * h + 1:2 * h + 2])))

        def mm256(out_sb, out_off, lhs_sb, lhs_off, rhs_sb, rhs_off, dep0,
                  fuse=None):
            evs = []
            for mo_ in range(2):
                ps = pn_y1 if mo_ == 0 else pn_y2
                mmm = None
                for kk_ in range(2):
                    mmm = O("tensor", lambda e, mo_=mo_, kk_=kk_, ps=ps: e.matmul(
                        ps[:, :],
                        lhs_sb[:, lhs_off + kk_ * 256 + mo_ * 128:lhs_off + kk_ * 256 + (mo_ + 1) * 128],
                        rhs_sb[:, rhs_off + kk_ * 256:rhs_off + (kk_ + 1) * 256],
                        start=(kk_ == 0), stop=(kk_ == 1)),
                        deps=(dep0 + (evs[-2:-1] or [])) if (mo_ == 0 and kk_ == 0)
                        else ([evs[-2]] if (mo_ == 1 and kk_ == 0 and len(evs) >= 2)
                              else []))
                if fuse is None:
                    evs.append(O("vector", lambda e, mo_=mo_, ps=ps: e.tensor_copy(
                        out_sb[:, out_off + mo_ * 256:out_off + (mo_ + 1) * 256],
                        ps[:, :]), deps=[mmm]))
                else:
                    cstt, alpha = fuse
                    evs.append(O("vector", lambda e, mo_=mo_, ps=ps, cstt=cstt,
                                 alpha=alpha: e.scalar_tensor_tensor(
                                     out_sb[:, out_off + mo_ * 256:out_off + (mo_ + 1) * 256],
                                     ps[:, :], alpha,
                                     cstt[:, mo_ * 256:(mo_ + 1) * 256],
                                     AL.mult, AL.add), deps=[mmm]))
            return evs

        inv_sts = []
        for j in range(4):
            bl0 = O("vector", lambda e, j=j: e.tensor_scalar_mul(
                a_nw[:, :], e2all[:, j * 512:(j + 1) * 512], hsel_s[:, 0:1]),
                deps=[e2_evs[12 + j], cst["hsel"]] +
                ([inv_sts[-1]] if inv_sts else []))
            bl = bl0
            for r in range(1, 4):
                hh = r * 4 + j
                O("vector", lambda e, hh=hh, r=r: e.tensor_scalar_mul(
                    x1_nw[:, :], e2all[:, hh * 512:(hh + 1) * 512],
                    hsel_s[:, r:r + 1]))
                bl = O("vector", lambda e: e.tensor_add(
                    a_nw[:, :], a_nw[:, :], x1_nw[:, :]))
            rs1 = O("vector", lambda e: e.reduce_sum(
                nrm_s[:, 0:1], a_nw[:, 0:256], axis=AX.X))
            rs2 = O("vector", lambda e: e.reduce_sum(
                nrm_s[:, 1:2], a_nw[:, 256:512], axis=AX.X))
            cs_mm = None
            for mo in range(2):
                for kk in range(2):
                    cs_mm = O("tensor", lambda e, mo=mo, kk=kk: e.matmul(
                        pn_cs[:, mo:mo + 1],
                        a_nw[:, kk * 256 + mo * 128:kk * 256 + (mo + 1) * 128],
                        onec_b[:, :], start=(kk == 0), stop=(kk == 1)),
                        deps=[bl, onec] if (mo == 0 and kk == 0) else [])
            cs_ev = O("vector", lambda e: e.tensor_copy(
                nrm_s[:, 2:4], pn_cs[:, :]), deps=[cs_mm])
            g1_ = O("gpsimd", lambda e: e.dma_start(
                out=nrow_s[0:1, 0:256], in_=nrm_s[:, 0:2]),
                deps=[rs1, rs2, cs_ev], dma=True)
            g2_ = O("gpsimd", lambda e: e.dma_start(
                out=nrow2_s[0:1, 0:256], in_=nrm_s[:, 2:4]), dma=True)
            mx1 = O("vector", lambda e: e.reduce_max(
                nrow_s[0:1, 256:257], nrow_s[0:1, 0:256], axis=AX.X),
                deps=[g1_, g2_])
            mx2 = O("vector", lambda e: e.reduce_max(
                nrow2_s[0:1, 256:257], nrow2_s[0:1, 0:256], axis=AX.X))
            g3_ = O("gpsimd", lambda e: e.dma_start(
                out=nrow_s[0:1, 257:258], in_=nrow2_s[0:1, 256:257]),
                deps=[mx1, mx2], dma=True)
            O("vector", lambda e: e.tensor_mul(
                nrow_s[0:1, 258:259], nrow_s[0:1, 256:257], nrow_s[0:1, 257:258]),
                deps=[g3_])
            O("vector", lambda e: e.reciprocal(
                nrow_s[0:1, 259:260], nrow_s[0:1, 258:259]))
            rc_ev = O("vector", lambda e, j=j: e.tensor_copy(
                rcc_b[0:1, j:j + 1], nrow_s[0:1, 259:260]))
            rc_mm = O("tensor", lambda e, j=j: e.matmul(
                pn_rc[:, :], ones_b[0:1, :], rcc_b[0:1, j:j + 1],
                start=True, stop=True), deps=[rc_ev, cst["ones"]])
            z0_ev = None
            for kk in range(2):
                for mo in range(2):
                    dd = [rc_mm] if (kk == 0 and mo == 0) else [z0_ev]
                    tr = O("tensor", lambda e, kk=kk, mo=mo: e.transpose(
                        pn_t[:, :],
                        a_nw[:, kk * 256 + mo * 128:kk * 256 + (mo + 1) * 128],
                        id_b[:, :]), deps=dd)
                    z0_ev = O("vector", lambda e, kk=kk, mo=mo: e.tensor_scalar_mul(
                        z_nw[:, mo * 256 + kk * 128:mo * 256 + kk * 128 + 128],
                        pn_t[:, :], pn_rc[:, 0:1]), deps=[tr])
            s0_ev = None
            for kk in range(2):
                s0_ev = O("vector", lambda e, kk=kk: e.tensor_scalar_mul(
                    s_nw[:, kk * 256:(kk + 1) * 256],
                    a_nw[:, kk * 256:(kk + 1) * 256], pn_rc[:, 0:1]),
                    deps=[z0_ev] if kk == 0 else [])

            prev = [z0_ev, s0_ev]
            for it in range(ITERS):
                ev_y = mm256(y_nw, 0, s_nw, 0, a_nw, 0, prev)
                ev_x1 = mm256(x1_nw, 0, a_nw, 0, s_nw, 0, [ev_y[-1]],
                              fuse=(i7_s, -1.0))
                ev_x2 = mm256(x2_nw, 0, y_nw, 0, x1_nw, 0, [ev_x1[-1]],
                              fuse=(i15_s, -1.0))
                ev_pt = mm256(pt_nw, 0, y_nw, 0, x2_nw, 0, [ev_x2[-1]],
                              fuse=(i13_s, -0.25))
                last = it == ITERS - 1
                ev_sn = mm256(x1_nw, 0, z_nw, 0, pt_nw, 0, [ev_pt[-1]])
                if not last:
                    ev_zn = mm256(x2_nw, 0, pt_nw, 0, z_nw, 0, [ev_sn[-1]])
                    O("vector", lambda e: e.tensor_copy(s_nw[:, :], x1_nw[:, :]),
                      deps=[ev_zn[-1]])
                    prev = [O("vector", lambda e: e.tensor_copy(
                        z_nw[:, :], x2_nw[:, :]))]
                else:
                    prev = ev_sn
            inv_sts.append(O("sync", lambda e, j=j: e.dma_start(
                out=invin_d[j * 256:(j + 1) * 256, :]
                .rearrange("(kk p) c -> p kk c", p=128),
                in_=x1_nw.ap().rearrange("p (kk c) -> p kk c", c=256)),
                deps=[prev[-1]], dma=True))
        cc_inv = O("gpsimd", lambda e: e.collective_compute(
            "AllGather", AL.bypass, replica_groups=GROUPS,
            ins=[invin_d.ap().opt()], outs=[invout_d.ap().opt()]),
            deps=[inv_sts[-1]])
        if _stop("sn", blk, [cc_inv]):
            return nc, dbg
        P.flush(blk)

    with (
        nc.psum_tensor("p2_a", [128, 512], F32) as p2_a,
        nc.psum_tensor("p2_b", [128, 512], F32) as p2_b,
        nc.psum_tensor("p2_q1", [128, 512], F32) as p2_q1,
        nc.psum_tensor("p2_q2", [128, 512], F32) as p2_q2,
        nc.psum_tensor("p2_z0", [128, 65], F32) as p2_z0,
        nc.psum_tensor("p2_z1", [128, 65], F32) as p2_z1,
        nc.psum_tensor("p2_z2", [128, 65], F32) as p2_z2,
        nc.psum_tensor("p2_z3", [128, 65], F32) as p2_z3,
        nc.Block("s2") as blk,
    ):
        P.serial = False
        z1_init = O("vector", lambda e: e.memset(z1a[:, :], 0.0))
        zps = [p2_z0, p2_z1, p2_z2, p2_z3]
        q_sts, xh_sts, h1_anchor = [], [], []
        last_exp = None
        last_z1mm = [None] * 4
        last_z1add = [None] * 4
        prev_tail = [lm_done, inv_sts[-1], cc_inv]
        last_dp_reader = None
        evac_hist = []
        for ci, (t0, ntl) in enumerate(CHUNKS):
            W = ntl * 128
            is_c = ntl == 4
            ev1 = ev2 = None
            idx = 0
            for tl in range(ntl):
                t = t0 + tl
                dp_dep = [stats_done, cst["idb"]] + prev_tail
                if last_dp_reader is not None:
                    dp_dep.append(last_dp_reader)
                O("vector", lambda e, t=t: e.tensor_scalar_mul(
                    dpair[:, 0:128], id_b[:, :], stc(IX_R1, t)), deps=dp_dep)
                dp2 = O("vector", lambda e, t=t: e.tensor_scalar_mul(
                    dpair[:, 128:256], id_b[:, :], stc(IX_R12, t)))
                for fi in range(8):
                    ps = p2_a if idx % 2 == 0 else p2_b
                    depm = [dp2, rows_ev] + prev_tail
                    if len(evac_hist) >= 2:
                        depm += list(evac_hist[-2])
                    O("tensor", lambda e, t=t, fi=fi, ps=ps: e.matmul(
                        ps[:, 0:256],
                        x_bf[:, t * 1024 + fi * 128:t * 1024 + (fi + 1) * 128],
                        dpair[:, :], start=True, stop=False), deps=depm)
                    O("tensor", lambda e, t=t, ps=ps: e.matmul(
                        ps[:, 0:128], ones_b[0:1, :],
                        rows1_b[0:1, (2 * t) * 128:(2 * t + 1) * 128],
                        start=False, stop=False))
                    mm = O("tensor", lambda e, t=t, ps=ps: e.matmul(
                        ps[:, 128:256], ones_b[0:1, :],
                        rows1_b[0:1, (2 * t + 1) * 128:(2 * t + 2) * 128],
                        start=False, stop=True))
                    ev1 = O("scalar", lambda e, tl=tl, fi=fi, ps=ps, W=W:
                            e.activation(
                                xn_s[:, fi * W + tl * 128:fi * W + (tl + 1) * 128],
                                ps[:, 0:128], AF.Copy), deps=[mm])
                    ev2 = O("vector", lambda e, tl=tl, fi=fi, ps=ps, W=W:
                            e.tensor_copy(
                                xh_s[:, fi * W + tl * 128:fi * W + (tl + 1) * 128],
                                ps[:, 128:256]), deps=[ev1])
                    evac_hist.append((ev1, ev2))
                    idx += 1
                    if ci == 0 and tl == 0 and fi == 0 and _stop("s2t0", blk, [ev1, ev2]):
                        return nc, dbg
                last_dp_reader = mm
                if ci == 0 and tl == 0 and _stop("s2t1", blk, [ev1, ev2]):
                    return nc, dbg
            xn_done = [ev1, ev2]
            if ci == 0 and _stop("s2a", blk, xn_done):
                return nc, dbg
            if is_c:
                xh_dep = xn_done + ([xh_sts[-1]] if xh_sts else [])
                xh_sts.append(O("sync", lambda e, t0=t0: e.dma_start(
                    out=xh_d[:, (t0 - 1) * 128:(t0 + 3) * 128]
                    .rearrange("(fi p) c -> p fi c", p=128),
                    in_=xh_s.ap().rearrange("p (fi c) -> p fi c", c=512)),
                    deps=xh_dep, dma=True))

            qev = []
            for m in range(16):
                if m >= 8 and not is_c:
                    continue
                ps = p2_q1 if len(qev) % 2 == 0 else p2_q2
                col0 = (m % 8) * 128 + (1024 if m >= 8 else 0)
                mm = None
                for fi in range(8):
                    depm = []
                    if fi == 0:
                        depm = list(xn_done) + [wqkv_ld] + prev_tail
                        if len(qev) >= 2:
                            depm.append(qev[-2])
                    mm = O("tensor", lambda e, fi=fi, ps=ps, col0=col0, W=W:
                           e.matmul(ps[:, 0:W],
                                    wqkv_s[:, fi * 3072 + col0:fi * 3072 + col0 + 128],
                                    xn_s[:, fi * W:(fi + 1) * W],
                                    start=(fi == 0), stop=(fi == 7)), deps=depm)
                if m < 8:
                    evd = [mm] + ([q_sts[-2]] if len(q_sts) >= 2 else [])
                    ev = O("scalar", lambda e, m=m, ps=ps, W=W: e.activation(
                        stg_s[:, (m % 2) * 512:(m % 2) * 512 + W], ps[:, 0:W],
                        AF.Copy, scale=QSCALE), deps=evd)
                    q_sts.append(O("sync", lambda e, m=m, t0=t0, W=W: e.dma_start(
                        out=q_d[m * 128:(m + 1) * 128, t0 * 128:t0 * 128 + W],
                        in_=stg_s[0:128, (m % 2) * 512:(m % 2) * 512 + W]),
                        deps=[ev], dma=True))
                    qev.append(ev)
                else:
                    qev.append(O("vector", lambda e, m=m, ps=ps, W=W: e.tensor_copy(
                        k_s[:, (m - 8) * W:(m - 7) * W], ps[:, 0:W]), deps=[mm]))
            k_done = qev[-1]
            if ci == 0 and _stop("s2b", blk, [k_done, q_sts[-1]]):
                return nc, dbg

            v_evs = []
            lg_ev = None
            msk_done = []
            if is_c:
                vz = O("vector", lambda e: e.memset(
                    v_s.ap().rearrange("p (tl h c) -> p tl h c", h=16, c=65)
                    [:, :, :, 64:65], 1.0), deps=xn_done + prev_tail)
                for tl in range(4):
                    for half in range(2):
                        ps = p2_q1 if len(qev) % 2 == 0 else p2_q2
                        mm = None
                        for fi in range(8):
                            depm = []
                            if fi == 0:
                                depm = [qev[-2]]
                            mm = O("tensor", lambda e, tl=tl, half=half, fi=fi,
                                   ps=ps: e.matmul(
                                       ps[:, :],
                                       xn_s[:, fi * 512 + tl * 128:fi * 512 + (tl + 1) * 128],
                                       wqkv_s[:, fi * 3072 + 2048 + half * 512:fi * 3072 + 2048 + (half + 1) * 512],
                                       start=(fi == 0), stop=(fi == 7)), deps=depm)
                        for hh in range(8):
                            h = half * 8 + hh
                            dep_ev = [mm, vz] if hh == 0 else []
                            v_evs.append(O("vector",
                                           lambda e, tl=tl, h=h, hh=hh, ps=ps:
                                           e.tensor_copy(
                                               v_s[:, tl * 1040 + h * 65:tl * 1040 + h * 65 + 64],
                                               ps[:, hh * 64:(hh + 1) * 64]),
                                           deps=dep_ev))
                        qev.append(v_evs[-1])
                if ci == 0 and _stop("s2v", blk, [v_evs[-1]]):
                    return nc, dbg
                lg_mm = None
                for fi in range(8):
                    depm = [wgT_done, qev[-1]] if fi == 0 else []
                    lg_mm = O("tensor", lambda e, fi=fi: e.matmul(
                        p2_a[0:8, :], wgT_s[:, fi * 8:(fi + 1) * 8],
                        xn_s[:, fi * 512:(fi + 1) * 512],
                        start=(fi == 0), stop=(fi == 7)), deps=depm)
                lg_ev = O("vector", lambda e: e.tensor_copy(
                    lgt_s[0:8, :], p2_a[0:8, :]), deps=[lg_mm])
                P.serial = True
                for tl in range(4):
                    t = t0 + tl
                    trd = [lg_ev] if tl == 0 else [msk_done[-1]]
                    tr1 = O("tensor", lambda e, tl=tl: e.transpose(
                        p2_b[0:128, 0:8], lgt_s[0:8, tl * 128:(tl + 1) * 128],
                        id_f[0:8, 0:8]), deps=trd)
                    O("vector", lambda e: e.tensor_copy(
                        lgtT_s[:, 0:8], p2_b[0:128, 0:8]), deps=[tr1])
                    O("vector", lambda e: e.reduce_max(
                        lgtT_s[:, 8:9], lgtT_s[:, 0:8], axis=AX.X))
                    c3 = O("vector", lambda e: e.tensor_single_scalar(
                        mtm_s[:, 0:8], lgtT_s[:, 0:8], lgtT_s[:, 8:9], AL.is_ge))
                    tr2 = O("tensor", lambda e: e.transpose(
                        p2_b[0:8, 0:128], mtm_s[:, 0:8], id_f[:, :]), deps=[c3])
                    msk_done.append(O("vector", lambda e, t=t: e.tensor_copy(
                        mask_f[0:8, (t - 1) * 128:t * 128], p2_b[0:8, 0:128]),
                        deps=[tr2]))
                if ci == 0 and _stop("s2c", blk, [msk_done[-1]]):
                    return nc, dbg
                P.serial = False
                for hp in range(8):
                    h0, h1_ = 2 * hp, 2 * hp + 1
                    exp_ops = []
                    for tp in range(2):
                        mm_last = None
                        for tl2 in range(2):
                            tl = 2 * tp + tl2
                            depm = [k_done, lm_done, v_evs[-1], msk_done[-1]]
                            if tp == 1:
                                depm = [exp_ops[-1], last_z1mm[3]]
                            elif hp > 0:
                                depm = [last_exp, last_z1mm[3]]
                            O("tensor", lambda e, h=h0, tl=tl, tl2=tl2: e.matmul(
                                p2_q1[:, tl2 * 256:(tl2 + 1) * 256],
                                k_s[0:64,
                                    (h // 2) * 512 + tl * 128:(h // 2) * 512 + (tl + 1) * 128],
                                qlT[0:64, (h // 2) * 256:(h // 2) * 256 + 256],
                                start=True, stop=True),
                                deps=depm if tl2 == 0 else [])
                            mm_last = O("tensor", lambda e, h=h1_, tl=tl, tl2=tl2:
                                        e.matmul(
                                p2_q2[:, tl2 * 256:(tl2 + 1) * 256],
                                k_s[64:128,
                                    (h // 2) * 512 + tl * 128:(h // 2) * 512 + (tl + 1) * 128],
                                qlT[64:128, (h // 2) * 256:(h // 2) * 256 + 256],
                                start=True, stop=True))
                        exp_ops.append(O("scalar", lambda e: e.activation(
                            exp3_s[:, 0:512], p2_q1[:, :], AF.Exp),
                            deps=[mm_last]))
                        exp_ops.append(O("scalar", lambda e: e.activation(
                            exp3_s[:, 512:1024], p2_q2[:, :], AF.Exp)))
                        for tl2 in range(2):
                            tl = 2 * tp + tl2
                            for hh in range(2):
                                for mo in range(2):
                                    zi = hh * 2 + mo
                                    depz = [exp_ops[-1]]
                                    if tl == 0:
                                        depz.append(last_z1add[zi])
                                        depz.append(v_evs[-1])
                                    last_z1mm[zi] = O(
                                        "tensor", lambda e, hh=hh, mo=mo, tl=tl,
                                        tl2=tl2, hp=hp, zi=zi: e.matmul(
                                            zps[zi][:, :],
                                            exp3_s[:, hh * 512 + tl2 * 256 + mo * 128:
                                                   hh * 512 + tl2 * 256 + (mo + 1) * 128],
                                            v_s[:, tl * 1040 + (2 * hp + hh) * 65:
                                                tl * 1040 + (2 * hp + hh) * 65 + 65],
                                            start=(tl == 0), stop=(tl == 3)),
                                        deps=depz)
                    last_exp = exp_ops[-1]
                    for zi in range(4):
                        hh, mo = zi // 2, zi % 2
                        h = 2 * hp + hh
                        last_z1add[zi] = O(
                            "vector", lambda e, h=h, mo=mo, zi=zi:
                            e.scalar_tensor_tensor(
                                z1a[:, (h * 2 + mo) * 65:(h * 2 + mo + 1) * 65],
                                zps[zi][:, :], 1.0,
                                z1a[:, (h * 2 + mo) * 65:(h * 2 + mo + 1) * 65],
                                AL.mult, AL.add),
                            deps=[last_z1mm[zi], z1_init])
            prev_tail = [qev[-1]]
            last_wqkv_mm = mm
            if is_c:
                prev_tail += [last_z1add[3], v_evs[-1], msk_done[-1], last_exp]
                mask_all_done = msk_done[-1]
        if dbg_t("z1p", [128, H * 2 * 65]):
            O("sync", lambda e: e.dma_start(out=dbg["z1p"][:, :], in_=z1a[:, :]),
              deps=[o for o in last_z1add if o is not None], dma=True)
        if dbg_t("mask", [8, TLOC * 128], BF16):
            O("sync", lambda e: e.dma_start(out=dbg["mask"][:, :],
                                            in_=mask_f[0:8, :]),
              deps=[mask_all_done], dma=True)
        P.serial = True
        z1_st = O("sync", lambda e: e.dma_start(
            out=z1in_d.ap().rearrange("(h mo p) c -> p h mo c", p=128, mo=2),
            in_=z1a.ap().rearrange("p (h mo c) -> p h mo c", mo=2, c=65)),
            deps=[o for o in last_z1add if o is not None], dma=True)
        cc_z1 = O("gpsimd", lambda e: e.collective_compute(
            "AllReduce", AL.add, replica_groups=GROUPS,
            ins=[z1in_d.ap().opt()], outs=[z1out_d.ap().opt()]), deps=[z1_st])
        if _stop("s2", blk, [cc_z1, q_sts[-1], xh_sts[-1], mask_all_done]):
            return nc, dbg
        P.flush(blk)

    with (
        nc.psum_tensor("p3_h1", [128, 512], F32) as p3_h1,
        nc.psum_tensor("p3_h2", [128, 512], F32) as p3_h2,
        nc.psum_tensor("p3_mb", [128, 512], F32) as p3_mb,
        nc.psum_tensor("p3_y1", [128, 512], F32) as p3_y1,
        nc.psum_tensor("p3_y2", [128, 512], F32) as p3_y2,
        nc.Block("s3") as blk,
    ):
        P.serial = False
        w2_ld = O("gpsimd", lambda e: e.dma_start(
            out=w2_s.ap().rearrange("p (e mo c) -> p e mo c", mo=2, c=1024),
            in_=w2.ap().rearrange("e (mo p) c -> p e mo c", p=128)),
            deps=[csb_ld, last_wqkv_mm, last_exp], dma=True)
        y_sts = []
        h1_last = {0: None, 1: None}
        xh_readers_tail = None
        mb_reader = None
        yev_hist = []
        for ci in range(4):
            t0 = CHUNKS[ci][0]
            xh_dep = [xh_sts[ci]]
            if xh_readers_tail is not None:
                xh_dep.append(xh_readers_tail)
            xh_ld = O("scalar", lambda e, t0=t0: e.dma_start(
                out=xh_s.ap().rearrange("p (fi c) -> p fi c", c=512),
                in_=xh_d[:, (t0 - 1) * 128:(t0 + 3) * 128]
                .rearrange("(fi p) c -> p fi c", p=128)),
                deps=xh_dep, dma=True)
            w1p_lds = []
            gl_ops = []
            last_h1 = None
            for e_ in range(E):
                dd = [w1p_st[e_]]
                if e_ >= 2:
                    dd.append(h1_last[e_ % 2])
                elif ci > 0:
                    dd.append(h1_last[e_ % 2])
                w1p_lds.append(O("scalar", lambda e, e_=e_: e.dma_start(
                    out=w1p_s[:, (e_ % 2) * 2048:(e_ % 2) * 2048 + 2048]
                    .rearrange("p (fi c) -> p fi c", c=256),
                    in_=w1p_d[e_ * 1024:(e_ + 1) * 1024, :]
                    .rearrange("(fi p) c -> p fi c", p=128)), deps=dd, dma=True))
                mb_dep = [mask_all_done]
                if mb_reader is not None:
                    mb_dep.append(mb_reader)
                mb = O("tensor", lambda e, e_=e_, t0=t0: e.matmul(
                    p3_mb[:, :], sel8_s[0:8, e_ * 128:(e_ + 1) * 128],
                    mask_f[0:8, (t0 - 1) * 128:(t0 + 3) * 128],
                    start=True, stop=True), deps=mb_dep + [cst["sel8"]])
                base = (e_ % 2) * 2048
                for mo in range(2):
                    ps = p3_h1 if mo == 0 else p3_h2
                    mm = None
                    for fi in range(8):
                        depm = []
                        if fi == 0:
                            depm = [w1p_lds[-1], xh_ld]
                            if gl_ops:
                                depm.append(gl_ops[-1 if mo == 1 else -2]
                                            if len(gl_ops) >= 2 else gl_ops[-1])
                        mm = O("tensor", lambda e, base=base, mo=mo, fi=fi, ps=ps:
                               e.matmul(
                                   ps[:, :],
                                   w1p_s[:, base + fi * 256 + mo * 128:base + fi * 256 + (mo + 1) * 128],
                                   xh_s[:, fi * 512:(fi + 1) * 512],
                                   start=(fi == 0), stop=(fi == 7)), deps=depm)
                    gl = O("scalar", lambda e, e_=e_, mo=mo, ps=ps: e.activation(
                        mg_s[:, (e_ * 2 + mo) * 512:(e_ * 2 + mo + 1) * 512],
                        ps[:, :], AF.Gelu,
                        bias=cb_s[:, 2 * e_ + mo:2 * e_ + mo + 1]),
                        deps=[mm, ce_ev_prev] + (yev_hist[-1:] if ci > 0 and e_ == 0
                                                 and mo == 0 else []))
                    gl_ops.append(O("vector", lambda e, e_=e_, mo=mo: e.tensor_tensor(
                        mg_s[:, (e_ * 2 + mo) * 512:(e_ * 2 + mo + 1) * 512],
                        mg_s[:, (e_ * 2 + mo) * 512:(e_ * 2 + mo + 1) * 512],
                        p3_mb[:, :], AL.mult), deps=[gl, mb]))
                    h1_last[mo] = mm
                mb_reader = gl_ops[-1]
            xh_readers_tail = h1_last[1]
            h1_final = h1_last[1]
            for tl in range(4):
                t = t0 + tl
                for half in range(2):
                    ps = p3_y1 if half == 0 else p3_y2
                    mm = None
                    for e_ in range(E):
                        for mo in range(2):
                            depm = []
                            if e_ == 0 and mo == 0:
                                depm = [gl_ops[-1], w2_ld]
                                if len(yev_hist) >= 2:
                                    depm.append(yev_hist[-2])
                            mm = O("tensor", lambda e, e_=e_, mo=mo, tl=tl,
                                   half=half, ps=ps: e.matmul(
                                       ps[:, :],
                                       mg_s[:, (e_ * 2 + mo) * 512 + tl * 128:(e_ * 2 + mo) * 512 + (tl + 1) * 128],
                                       w2_s[:, (e_ * 2 + mo) * 1024 + half * 512:(e_ * 2 + mo) * 1024 + (half + 1) * 512],
                                       start=(e_ == 0 and mo == 0), stop=False),
                                   deps=depm)
                    mm = O("tensor", lambda e, t=t, half=half, ps=ps: e.matmul(
                        ps[:, :], mask_f[0:8, (t - 1) * 128:t * 128],
                        b2_s[0:8, half * 512:(half + 1) * 512],
                        start=False, stop=True), deps=[b2_ld])
                    ev = O("vector", lambda e, tl=tl, half=half, ps=ps:
                           e.tensor_copy(
                               stg_s[:, (tl % 2) * 1024 + half * 512:(tl % 2) * 1024 + (half + 1) * 512],
                               ps[:, :]),
                           deps=[mm] + ([y_sts[-2]] if len(y_sts) >= 2
                                        else [q_sts[-1]]))
                    yev_hist.append(ev)
                    if half == 1:
                        y_sts.append(O("sync", lambda e, t=t, tl=tl: e.dma_start(
                            out=y_d[(t - 1) * 128:t * 128, :],
                            in_=stg_s[:, (tl % 2) * 1024:(tl % 2) * 1024 + 1024]),
                            deps=[ev], dma=True))
        if _stop("s3", blk, [y_sts[-1]]):
            return nc, dbg
        P.flush(blk)

    with (
        nc.psum_tensor("p5_z", [128, 64], F32) as p5_z,
        nc.psum_tensor("p5_z2", [128, 64], F32) as p5_z2,
        nc.Block("s5") as blk,
    ):
        P.serial = False
        z1_ld = O("gpsimd", lambda e: e.dma_start(
            out=z1a.ap().rearrange("p (h mo c) -> p h mo c", mo=2, c=65),
            in_=z1out_d.ap().rearrange("(h mo p) c -> p h mo c", p=128, mo=2)),
            deps=[cc_z1], dma=True)
        inv_ld = O("gpsimd", lambda e: e.dma_start(
            out=invT_s.ap().rearrange("p (h kk c) -> p h kk c", kk=2, c=256),
            in_=invout_d.ap().rearrange("(h kk p) c -> p h kk c", p=128, kk=2)),
            deps=[cc_inv, h1_final], dma=True)
        zb_ones = O("vector", lambda e: e.memset(
            zaug.ap().rearrange("p (hm c) -> p hm c", c=65)[:, :, 64:65], 1.0))
        a3_done = []
        for h in range(H):
            for kk in range(2):
                col = (h % 4) * 2 + kk
                rdep = [z1_ld] + ([a3_done[2 * (h - 4) + kk]] if h >= 4 else [])
                r1_ = O("vector", lambda e, h=h, kk=kk, col=col: e.reciprocal(
                    rec6_s[:, col:col + 1],
                    z1a[:, (h * 2 + kk) * 65 + 64:(h * 2 + kk) * 65 + 65]),
                    deps=rdep)
                a3_done.append(O("vector", lambda e, h=h, kk=kk, col=col:
                                 e.tensor_scalar_mul(
                    a3v_s[:, (h * 2 + kk) * 64:(h * 2 + kk + 1) * 64],
                    z1a[:, (h * 2 + kk) * 65:(h * 2 + kk) * 65 + 64],
                    rec6_s[:, col:col + 1]), deps=[r1_]))
        z_evs = []
        for h in range(H):
            for mo in range(2):
                ps = p5_z if mo == 0 else p5_z2
                mm = None
                for kk in range(2):
                    depm = []
                    if kk == 0:
                        depm = [a3_done[2 * h + 1], inv_ld]
                        if len(z_evs) >= 2:
                            depm.append(z_evs[-2])
                    mm = O("tensor", lambda e, h=h, mo=mo, kk=kk, ps=ps: e.matmul(
                        ps[:, :],
                        invT_s[:, h * 512 + kk * 256 + mo * 128:h * 512 + kk * 256 + (mo + 1) * 128],
                        a3v_s[:, (h * 2 + kk) * 64:(h * 2 + kk + 1) * 64],
                        start=(kk == 0), stop=(kk == 1)), deps=depm)
                z_evs.append(O("vector", lambda e, h=h, mo=mo, ps=ps: e.tensor_copy(
                    zaug[:, (h * 2 + mo) * 65:(h * 2 + mo) * 65 + 64], ps[:, :]),
                    deps=[mm, zb_ones]))
        zaug_done = z_evs[-1]
        if dbg_t("zaug", [128, H * 2 * 65], BF16):
            O("sync", lambda e: e.dma_start(out=dbg["zaug"][:, :],
                                            in_=zaug[:, :]), deps=[zaug_done], dma=True)
        if dbg_t("z1a", [128, H * 2 * 65]):
            O("sync", lambda e: e.dma_start(out=dbg["z1a"][:, :],
                                            in_=z1a[:, :]), deps=[z1_ld], dma=True)
        if _stop("s5", blk, [zaug_done]):
            return nc, dbg
        P.flush(blk)

    with (
        nc.psum_tensor("p6_s1", [128, 1024], F32) as p6_s1,
        nc.psum_tensor("p6_s1b", [128, 1024], F32) as p6_s1b,
        nc.psum_tensor("p6_o0", [128, 65], F32) as p6_o0,
        nc.psum_tensor("p6_o1", [128, 65], F32) as p6_o1,
        nc.psum_tensor("p6_o2", [128, 65], F32) as p6_o2,
        nc.psum_tensor("p6_o3", [128, 65], F32) as p6_o3,
        nc.Block("s6") as blk,
    ):
        wfc_ld = O("gpsimd", lambda e: e.dma_start(
            out=wfc_s.ap().rearrange("p (fi c) -> p fi c", c=1024),
            in_=wfc.ap().rearrange("(fi p) c -> p fi c", p=128)),
            deps=[yev_hist[-1]], dma=True)
        pso = [p6_o0, p6_o1, p6_o2, p6_o3]
        on_last = None
        exp_par = {0: None, 1: None}
        o_readers = {0: None, 1: None, 2: None, 3: None}
        chunk_sim1 = []
        exp_prev = None
        for ci, (t0, ntl) in enumerate(CHUNKS):
            W = ntl * 128
            sl = (ci % 2) * 4096
            q_dep = [q_sts[-1], zaug_done, yev_hist[-1]]
            if len(chunk_sim1) >= 2:
                q_dep.append(chunk_sim1[-2])
            q_ld = O("scalar", lambda e, t0=t0, W=W, sl=sl: e.dma_start(
                out=qc_s[:, sl:sl + 8 * W].rearrange("p (fi c) -> p fi c", c=W),
                in_=q_d[:, t0 * 128:t0 * 128 + W]
                .rearrange("(fi p) c -> p fi c", p=128)), deps=q_dep, dma=True)
            mm2 = None
            for h in range(H):
                po = (h % 2) * 64
                fj = h // 2
                ps1 = p6_s1 if h % 2 == 0 else p6_s1b
                for mo in range(2):
                    dd = []
                    if mo == 0:
                        dd = [q_ld, zaug_done]
                        if exp_par.get(h % 2) is not None:
                            dd.append(exp_par[h % 2])
                        elif exp_prev is not None:
                            dd.append(exp_prev)
                    mm2 = O("tensor", lambda e, po=po, fj=fj, mo=mo, W=W, ps1=ps1,
                            sl=sl: e.matmul(
                        ps1[:, mo * 512:mo * 512 + W],
                        klT[po:po + 64, fj * 256 + mo * 128:fj * 256 + (mo + 1) * 128],
                        qc_s[po:po + 64, sl + fj * W:sl + (fj + 1) * W],
                        start=True, stop=True), deps=dd)
                ex = None
                for mo in range(2):
                    ex = O("scalar", lambda e, mo=mo, W=W, ps1=ps1: e.activation(
                        exp1_s[:, mo * 512:mo * 512 + W],
                        ps1[:, mo * 512:mo * 512 + W], AF.Exp),
                        deps=[mm2] if mo == 0 else [])
                exp_par[h % 2] = ex
                exp_prev = ex
                for tl in range(ntl):
                    t = t0 + tl
                    zi = tl % 4
                    omm = None
                    for mo in range(2):
                        dd = []
                        if mo == 0:
                            dd = [ex]
                            if o_readers[zi] is not None:
                                dd.append(o_readers[zi])
                        omm = O("tensor", lambda e, mo=mo, tl=tl, h=h, zi=zi:
                                e.matmul(
                                    pso[zi][:, :],
                                    exp1_s[:, mo * 512 + tl * 128:mo * 512 + (tl + 1) * 128],
                                    zaug[:, (h * 2 + mo) * 65:(h * 2 + mo + 1) * 65],
                                    start=(mo == 0), stop=(mo == 1)), deps=dd)
                    rc = O("vector", lambda e, zi=zi: e.reciprocal(
                        rec6_s[:, zi:zi + 1], pso[zi][:, 64:65]), deps=[omm])
                    if ntl == 1:
                        hcol = 0 if t0 == 0 else 1
                        rc = O("vector", lambda e, zi=zi, hcol=hcol: e.tensor_mul(
                            rec6_s[:, zi:zi + 1], rec6_s[:, zi:zi + 1],
                            hm_s[:, hcol:hcol + 1]), deps=[cst["hm"], rc])
                    on_last = O("vector", lambda e, t=t, h=h, zi=zi:
                                e.tensor_scalar_mul(
                                    on_s[:, t * 1024 + h * 64:t * 1024 + (h + 1) * 64],
                                    pso[zi][:, 0:64], rec6_s[:, zi:zi + 1]),
                                deps=[rc, h1_final])
                    o_readers[zi] = on_last
            chunk_sim1.append(mm2)
        on_done = on_last
        if dbg_t("on", [128, NT * 1024], BF16):
            O("sync", lambda e: e.dma_start(out=dbg["on"][:, :],
                                            in_=on_s[:, :]), deps=[on_done], dma=True)
        if _stop("s6", blk, [on_done]):
            return nc, dbg
        P.flush(blk)

    with (
        nc.psum_tensor("p7_ca", [64, 128], F32) as p7_ca,
        nc.psum_tensor("p7_cb", [64, 128], F32) as p7_cb,
        nc.psum_tensor("p7_f0", [128, 512], F32) as p7_f0,
        nc.psum_tensor("p7_f1", [128, 512], F32) as p7_f1,
        nc.Block("s7") as blk,
    ):
        out_dmas = []
        gp_hist = []
        fc_hist = []
        r_readers = None
        conv_readers = {0: None, 1: None}
        yb_hist = []
        for t in range(1, TLOC + 1):
            par = t % 2
            ybd = [y_sts[t - 1]]
            if len(gp_hist) >= 3:
                ybd.append(gp_hist[-3])
            yb_ld = O("scalar", lambda e, t=t, par=par: e.dma_start(
                out=yb_s[:, par * 1024:(par + 1) * 1024],
                in_=y_d[(t - 1) * 128:t * 128, :]), deps=ybd, dma=True)
            yb_hist.append(yb_ld)
            r_evs = []
            for fpair in range(8):
                h0, h1_ = 2 * fpair, 2 * fpair + 1
                for hh, (h, pc) in enumerate([(h0, p7_ca), (h1_, p7_cb)]):
                    mm = None
                    for dlt in (-1, 0, 1):
                        dd = []
                        if dlt == -1:
                            dd = [on_done, csb_ld]
                            if conv_readers[hh] is not None:
                                dd.append(conv_readers[hh])
                        mm = O("tensor", lambda e, t=t, h=h, dlt=dlt, pc=pc:
                               e.matmul(
                                   pc[:, :],
                                   on_s[:, (t + dlt) * 1024 + h * 64:(t + dlt) * 1024 + (h + 1) * 64],
                                   c_sb[:, (h * 3 + dlt + 1) * 128:(h * 3 + dlt + 2) * 128],
                                   start=(dlt == -1), stop=(dlt == 1)), deps=dd)
                    ev = O("vector", lambda e, fpair=fpair, hh=hh, par=par, pc=pc:
                           e.tensor_copy(
                               r_s[hh * 64:(hh + 1) * 64,
                                   par * 1024 + fpair * 128:par * 1024 + (fpair + 1) * 128],
                               pc[:, :]), deps=[mm])
                    conv_readers[hh] = ev
                    r_evs.append(ev)
            r_done = r_evs[-1]
            for half in range(2):
                pf = p7_f0 if half == 0 else p7_f1
                mm = None
                for fi in range(8):
                    dd = []
                    if fi == 0:
                        dd = [r_done, wfc_ld]
                        if len(fc_hist) >= 2:
                            dd.append(fc_hist[-2])
                    mm = O("tensor", lambda e, fi=fi, half=half, par=par, pf=pf:
                           e.matmul(
                               pf[:, :],
                               r_s[:, par * 1024 + fi * 128:par * 1024 + (fi + 1) * 128],
                               wfc_s[:, fi * 1024 + half * 512:fi * 1024 + (half + 1) * 512],
                               start=(fi == 0), stop=False), deps=dd)
                mm = O("tensor", lambda e, half=half, pf=pf: e.matmul(
                    pf[:, :], ones_b[0:1, :], bfc_s[0:1, half * 512:(half + 1) * 512],
                    start=False, stop=True), deps=[bfc_ld])
                vadd = O("vector", lambda e, t=t, half=half, par=par, pf=pf:
                         e.scalar_tensor_tensor(
                             outs_s[:, par * 1024 + half * 512:par * 1024 + (half + 1) * 512],
                             pf[:, :], 1.0,
                             x_bf[:, t * 1024 + half * 512:t * 1024 + (half + 1) * 512],
                             AL.mult, AL.add),
                         deps=[mm] + ([out_dmas[-2]] if len(out_dmas) >= 2 else []))
                gp_hist.append(O("vector", lambda e, half=half, par=par:
                                 e.tensor_add(
                                     outs_s[:, par * 1024 + half * 512:par * 1024 + (half + 1) * 512],
                                     outs_s[:, par * 1024 + half * 512:par * 1024 + (half + 1) * 512],
                                     yb_s[:, par * 1024 + half * 512:par * 1024 + (half + 1) * 512]),
                                 deps=[vadd, yb_ld]))
                fc_hist.append(gp_hist[-1])
            out_dmas.append(O("sync", lambda e, t=t, par=par: e.dma_start(
                out=out[(t - 1) * 128:t * 128, :],
                in_=outs_s[:, par * 1024:(par + 1) * 1024]),
                deps=gp_hist[-2:], dma=True))
        O("sync", lambda e: e.nop(), deps=out_dmas)
        P.flush(blk)

    return nc, dbg


_CACHE = {}


def _build_cached():
    if "nc" not in _CACHE:
        _CACHE["nc"] = build_program()[0]
    return _CACHE["nc"]


def _prep_inputs(kw):
    x = np.ascontiguousarray(kw["L0_attn_x"], dtype=np.float32)
    wqkv = np.ascontiguousarray(kw["L0_attn_w_qkv"], dtype=np.float32)
    wconv = np.ascontiguousarray(kw["L0_attn_w_conv"].reshape(H, KW), dtype=np.float32)
    wfc = np.ascontiguousarray(kw["L0_attn_w_fc"], dtype=np.float32)
    bfc = np.ascontiguousarray(kw["L0_attn_b_fc"].reshape(1, D), dtype=np.float32)
    wg = np.ascontiguousarray(kw["L0_ffn_Wg"], dtype=np.float32)
    lns = np.ascontiguousarray(kw["L0_ffn_ln_scale"], dtype=np.float32)
    lnb = np.ascontiguousarray(kw["L0_ffn_ln_bias"], dtype=np.float32)
    w1 = np.ascontiguousarray(kw["L0_ffn_W1"], dtype=np.float32)
    b1 = np.ascontiguousarray(kw["L0_ffn_b1"], dtype=np.float32)
    w2 = np.ascontiguousarray(kw["L0_ffn_W2"], dtype=np.float32)
    b2 = np.ascontiguousarray(kw["L0_ffn_b2"], dtype=np.float32)
    in_maps = []
    for c in range(8):
        b, qtr = c // 4, c % 4
        xi = np.zeros((NT * 128, D), dtype=np.float32)
        lo, hi = qtr * TC - 128, (qtr + 1) * TC + 128
        slo, shi = max(lo, 0), min(hi, N)
        xi[slo - lo:slo - lo + (shi - slo)] = x[b, slo:shi]
        hm = np.zeros((128, 2), dtype=np.float32)
        hm[:, 0] = 1.0 if qtr > 0 else 0.0
        hm[:, 1] = 1.0 if qtr < 3 else 0.0
        hs = np.zeros((128, 4), dtype=np.float32)
        hs[:, qtr] = 1.0
        in_maps.append(dict(
            xin=xi, hmask=hm, hsel=hs, wqkv=wqkv, wconv=wconv, wfc=wfc,
            bfc=bfc, wg=wg, lns=lns, lnb=lnb, w1=w1, b1=b1, w2=w2, b2=b2))
    return in_maps


def kernel(**kw):
    nc = _build_cached()
    in_maps = _prep_inputs(kw)
    res = run_bass_kernel_spmd(nc, in_maps, core_ids=list(range(8)))
    y = np.empty((B, N, D), dtype=np.float32)
    for c in range(8):
        b, qtr = c // 4, c % 4
        y[b, qtr * TC:(qtr + 1) * TC, :] = res.results[c]["out"]
    return y


if __name__ == "__main__":
    nc, _ = build_program()
    print("instructions:", nc.n_instructions())
